# version 02
# baseline (speedup 1.0000x reference)
import numpy as np
from contextlib import ExitStack

import concourse.bass as bass
import concourse.mybir as mybir
from concourse.bass_utils import run_bass_kernel_spmd

F32 = mybir.dt.float32
AF = mybir.ActivationFunctionType
ALU = mybir.AluOpType

D = 4096
INTER = 14336
S = 2048
OWN = 1024
NH, NKV, HD = 16, 4, 64
LAT = NH * HD
KVD = NKV * HD
QKV = LAT + 2 * KVD
TSP = 512
NCH = D // 128

ENGINES = ("sync", "tensor", "vector", "scalar", "gpsimd")


class Ch:
    def __init__(self, sem, step):
        self.sem = sem
        self.step = step
        self.count = 0


class Prog:
    def __init__(self):
        self.ops = {e: [] for e in ENGINES}
        self._seen = {e: {} for e in ENGINES}
        self.self_ch = {}

    def emit(self, eng, fn, waits=(), inc=None):
        real = []
        seen = self._seen[eng]
        ch = self.self_ch.get(eng) if fn is not None else None
        if ch is not None:
            waits = list(waits) + [(ch.sem, ch.count)]
            inc = ch
        for w in waits:
            if w is None:
                continue
            sem, val = w
            if val <= 0 or seen.get(sem, -1) >= val:
                continue
            seen[sem] = val
            real.append((sem, val))
        incspec = None
        if inc is not None:
            inc.count += inc.step
            incspec = (inc.sem, inc.step)
        self.ops[eng].append((fn, tuple(real), incspec))
        return (inc.sem, inc.count) if inc is not None else None


def build_program():
    est = ExitStack()
    nc = bass.Bass("TRN2", target_bir_lowering=False)

    di = {}

    def inp(name, shape):
        di[name] = nc.dram_tensor(name, list(shape), F32, kind="ExternalInput").ap()
        return di[name]

    x_fm = inp("x_fm", (D, S))
    wqkvT = inp("wqkvT", (D, QKV))
    woT = inp("woT", (LAT, D))
    w1T = inp("w1T", (D, INTER))
    w3T = inp("w3T", (D, INTER))
    w2T = inp("w2T", (INTER, D))
    s1row = inp("s1row", (32, INTER))
    s3row = inp("s3row", (32, INTER))
    s2row = inp("s2row", (112, D))
    qc1L = inp("qc1L", (3, 8, 2, 128, 128))
    qc2L = inp("qc2L", (3, 8, 8, 128, 128))
    kc1L = inp("kc1L", (3, 2, 2, 128, 128))
    kc2L = inp("kc2L", (3, 2, 2, 128, 128))
    qbias = inp("qbias", (128, 16))
    kbias = inp("kbias", (128, 4))
    Mq = inp("Mq", (8, 128, 64))
    Mk = inp("Mk", (2, 128, 64))
    S5q = inp("S5q", (8, 64, 128))
    S5k = inp("S5k", (2, 64, 128))
    Oq = inp("Oq", (8, 128, 16))
    Ok = inp("Ok", (2, 128, 4))
    Bq = inp("Bq", (8, 16, 128))
    Bk = inp("Bk", (2, 4, 128))
    cosk = inp("cosk", (128, S))
    sink = inp("sink", (128, S))
    maskin = inp("maskin", (16, 128, OWN))
    ktemp = inp("ktemp", (128, 1))
    iodd = inp("iodd", (128, 1))
    ieven = inp("ieven", (128, 1))
    ident = inp("ident", (128, 128))

    y_out = nc.dram_tensor("y", [D, OWN], F32, kind="ExternalOutput").ap()

    wdq1 = nc.dram_tensor("wdq1", [D, INTER], F32).ap()
    wdq3 = nc.dram_tensor("wdq3", [D, INTER], F32).ap()
    wdq2 = nc.dram_tensor("wdq2", [INTER, D], F32).ap()
    x1 = nc.dram_tensor("x1d", [D, S], F32).ap()
    qkp = nc.dram_tensor("qkp", [QKV, S + 8], F32).ap()
    kfin = nc.dram_tensor("kfin", [KVD, S], F32).ap()
    qfin = nc.dram_tensor("qfin", [LAT, OWN], F32).ap()
    x2 = nc.dram_tensor("x2d", [D, OWN], F32).ap()

    def dcol(j):
        return 4 + j if j < OWN else 8 + j

    sbufs = {}

    def sbuf(name, shape, dtype=F32):
        sbufs[name] = est.enter_context(nc.sbuf_tensor(name, list(shape), dtype))
        return sbufs[name]

    AR1 = sbuf("AR1", (128, 16384))
    AR2 = sbuf("AR2", (128, 16384))
    AR3 = sbuf("AR3", (128, 8192))
    AR4 = sbuf("AR4", (128, 5504))
    PCon = sbuf("PCon", (128, 2624))
    PRow = sbuf("PRow", (1, 4096))
    ps = est.enter_context(nc.psum_tensor("ps", [128, 4096], F32))

    def bank(b, p0=0, p1=128, c0=0, c1=512):
        return ps[p0:p1, b * 512 + c0:b * 512 + c1]

    onesA = PCon[:, 0:1]
    epsc = PCon[0:1, 1:2]
    negM = PCon[:, 2:3]
    ktc = PCon[:, 3:4]
    ioddc = PCon[:, 4:5]
    ievenc = PCon[:, 5:6]
    C_QB, C_KB, C_ID, C_MQ, C_MK, C_S5Q, C_S5K, C_OQ, C_OK = \
        6, 22, 26, 154, 666, 794, 1818, 2074, 2202
    identt = PCon[:, C_ID:C_ID + 128]

    def qb_col(j):
        return PCon[:, C_QB + j:C_QB + j + 1]

    def kb_col(j):
        return PCon[:, C_KB + j:C_KB + j + 1]

    def mq_t(c):
        return PCon[:, C_MQ + c * 64:C_MQ + (c + 1) * 64]

    def mk_t(c):
        return PCon[:, C_MK + c * 64:C_MK + (c + 1) * 64]

    def s5q_t(c):
        return PCon[0:64, C_S5Q + c * 128:C_S5Q + (c + 1) * 128]

    def s5k_t(c):
        return PCon[0:64, C_S5K + c * 128:C_S5K + (c + 1) * 128]

    def oq_t(c):
        return PCon[:, C_OQ + c * 16:C_OQ + (c + 1) * 16]

    def ok_t(c):
        return PCon[:, C_OK + c * 4:C_OK + (c + 1) * 4]

    BQ0 = 5504 - 1024 - 256 - 128
    BK0 = BQ0 + 1024
    ON0 = BK0 + 256

    def bq_t(c):
        return AR4[0:16, BQ0 + c * 128:BQ0 + (c + 1) * 128]

    def bk_t(c):
        return AR4[0:4, BK0 + c * 128:BK0 + (c + 1) * 128]

    ones1r = AR4[0:1, ON0:ON0 + 128]

    semX = est.enter_context(nc.semaphore("cX"))
    semP = est.enter_context(nc.semaphore("cP"))
    semV = est.enter_context(nc.semaphore("cV"))
    semA = est.enter_context(nc.semaphore("cA"))
    cX = Ch(semX, 16)
    cP, cV, cA = Ch(semP, 1), Ch(semV, 1), Ch(semA, 1)
    P = Prog()
    P.self_ch = {"vector": cV, "scalar": cA}
    emit = P.emit

    DCHS = [cX]

    def dch(name, n=1):
        out = []
        for i in range(n):
            c = Ch(est.enter_context(nc.semaphore(f"{name}{i}")), 16)
            DCHS.append(c)
            out.append(c)
        return out if n > 1 else out[0]

    dS2 = dch("dS", 2)
    dW4 = dch("dW", 4)
    dQ4 = dch("dQ", 4)
    lnX = dch("lnX", 4)
    fw4 = dch("fw", 4)
    fw24 = dch("fw2", 4)
    resX = dch("rX", 2)
    resY = dch("rY", 2)
    dCT = dch("dCT")
    qp4 = dch("qp", 4)
    kp2 = dch("kp", 2)
    dcs = dch("dcs", 2)
    dvt = dch("dvt", 2)
    dQR = dch("dQR")
    qi8 = dch("qi", 8)
    dKS = dch("dKS")
    qs2 = dch("qs", 2)
    mk2 = dch("mk", 2)
    st2 = dch("st", 2)

    def at(ch):
        return (ch.sem, ch.count)

    def barrier():
        waits = [at(c) for c in DCHS] + [at(cP), at(cV), at(cA)]
        for e in ENGINES:
            emit(e, None, waits=waits)

    def ld(dst, src):
        return emit("gpsimd", lambda g, d=dst, s=src: g.dma_start(out=d, in_=s), inc=cX)

    ld(ktc, ktemp)
    ld(ioddc, iodd)
    ld(ievenc, ieven)
    ld(PCon[:, C_QB:C_QB + 16], qbias)
    ld(PCon[:, C_KB:C_KB + 4], kbias)
    ld(identt, ident)
    for c in range(8):
        ld(mq_t(c), Mq[c])
        ld(s5q_t(c), S5q[c])
        ld(oq_t(c), Oq[c])
        ld(bq_t(c), Bq[c])
    for c in range(2):
        ld(mk_t(c), Mk[c])
        ld(s5k_t(c), S5k[c])
        ld(ok_t(c), Ok[c])
        ld(bk_t(c), Bk[c])
    lds = at(cX)
    emit("vector", lambda v: v.memset(onesA, 1.0), waits=[lds])
    emit("vector", lambda v: v.memset(ones1r, 1.0))
    emit("vector", lambda v: v.memset(epsc, 1e-5))
    emit("vector", lambda v: v.tensor_scalar_mul(negM, ktc, -1.0))
    emit("vector", lambda v: v.tensor_tensor(negM, negM, ktc, ALU.max))
    emit("vector", lambda v: v.tensor_scalar_mul(negM, negM, -0.125), inc=cV)
    barrier()

    def dequant(wsrc, srow_d, wdst, KB, N):
        srow = [PRow[0:1, 0:2048], PRow[0:1, 2048:4096]]
        wq_ring = [AR1[:, i * 512:(i + 1) * 512] for i in range(4)]
        dq_ring = [AR2[:, i * 512:(i + 1) * 512] for i in range(4)]
        wq_free, dq_free, sr_free = {}, {}, {}
        nsp = N // 2048
        it = 0
        for kb in range(KB):
            for sp in range(nsp):
                si = kb * nsp + sp
                swl = [sr_free.get(si - 2)]
                ssnap = emit("gpsimd", lambda g, kb=kb, sp=sp, sl=si % 2: g.dma_start(
                    out=srow[sl], in_=srow_d[kb:kb + 1, sp * 2048:(sp + 1) * 2048]),
                    waits=swl, inc=dS2[si % 2])
                for j in range(4):
                    i = it
                    it += 1
                    wsnap = emit("sync", lambda s, kb=kb, sp=sp, j=j, sl=i % 4: s.dma_start(
                        out=wq_ring[sl],
                        in_=wsrc[kb * 128:(kb + 1) * 128, sp * 2048 + j * 512:sp * 2048 + (j + 1) * 512]),
                        waits=[wq_free.get(i - 4)], inc=dW4[i % 4])
                    psnap = emit("tensor", lambda t, j=j, sl=si % 2, bk=i % 2: t.matmul(
                        bank(bk), ones1r, srow[sl][0:1, j * 512:(j + 1) * 512],
                        start=True, stop=True),
                        waits=[ssnap, dq_free.get(i - 2)], inc=cP)
                    vsnap = emit("vector", lambda v, sl=i % 4, bk=i % 2: v.tensor_tensor(
                        dq_ring[sl], wq_ring[sl], bank(bk), ALU.mult),
                        waits=[wsnap, psnap, dq_free.get(i - 4)], inc=cV)
                    wq_free[i] = vsnap
                    stsnap = emit("gpsimd", lambda g, kb=kb, sp=sp, j=j, sl=i % 4: g.dma_start(
                        out=wdst[kb * 128:(kb + 1) * 128, sp * 2048 + j * 512:sp * 2048 + (j + 1) * 512],
                        in_=dq_ring[sl]),
                        waits=[vsnap], inc=dQ4[i % 4])
                    dq_free[i] = stsnap
                sr_free[si] = at(cP)

    dequant(w1T, s1row, wdq1, 32, INTER)
    barrier()
    dequant(w3T, s3row, wdq3, 32, INTER)
    barrier()
    dequant(w2T, s2row, wdq2, 112, D)
    barrier()

    def ln_span(xsrc):
        xb = [AR4[:, i * 512:(i + 1) * 512] for i in range(4)]
        sqb = [AR4[:, (4 + i) * 512:(5 + i) * 512] for i in range(2)]
        xb_free, sq_free, sq_of = {}, {}, {}
        for c in range(NCH):
            xs = emit("gpsimd", lambda g, c=c: g.dma_start(out=xb[c % 4], in_=xsrc(c)),
                      waits=[xb_free.get(c - 4), sq_of.get(c - 4)], inc=lnX[c % 4])
            sq = emit("scalar", lambda s, c=c: s.activation(sqb[c % 2], xb[c % 4], AF.Square),
                      waits=[xs, sq_free.get(c - 2)], inc=cA)
            sq_of[c] = sq
            m1 = emit("tensor", lambda t, c=c: t.matmul(bank(0, 0, 1), onesA, xb[c % 4],
                                                        start=(c == 0), stop=(c == NCH - 1)),
                      waits=[xs], inc=cP)
            xb_free[c] = m1
            m2 = emit("tensor", lambda t, c=c: t.matmul(bank(1, 0, 1), onesA, sqb[c % 2],
                                                        start=(c == 0), stop=(c == NCH - 1)),
                      waits=[sq], inc=cP)
            sq_free[c] = m2
        sums = at(cP)
        meanr = PRow[0:1, 0:512]
        msqr = PRow[0:1, 512:1024]
        varr = PRow[0:1, 1024:1536]
        stdr = PRow[0:1, 1536:2048]
        rstdr = PRow[0:1, 2048:2560]
        emit("vector", lambda v: v.tensor_scalar_mul(meanr, bank(0, 0, 1), 1.0 / D), waits=[sums])
        emit("vector", lambda v: v.tensor_tensor(msqr, meanr, meanr, ALU.mult))
        vs = emit("vector", lambda v: v.scalar_tensor_tensor(varr, bank(1, 0, 1), 1.0 / D, msqr,
                                                             ALU.mult, ALU.subtract), inc=cV)
        ss = emit("scalar", lambda s: s.activation(stdr, varr, AF.Sqrt, bias=epsc), waits=[vs], inc=cA)
        rs = emit("vector", lambda v: v.reciprocal(rstdr, stdr), waits=[ss], inc=cV)
        bm = emit("tensor", lambda t: t.matmul(bank(2), ones1r, meanr, start=True, stop=True),
                  waits=[vs], inc=cP)
        br = emit("tensor", lambda t: t.matmul(bank(3), ones1r, rstdr, start=True, stop=True),
                  waits=[rs], inc=cP)
        meanb = AR4[:, 3072:3584]
        rstdb = AR4[:, 3584:4096]
        e1 = emit("vector", lambda v: v.tensor_copy(meanb, bank(2)), waits=[bm], inc=cV)
        e2 = emit("vector", lambda v: v.tensor_copy(rstdb, bank(3)), waits=[br], inc=cV)
        xb_free2 = {}
        for c in range(NCH):
            xs = emit("gpsimd", lambda g, c=c: g.dma_start(out=xb[c % 4], in_=xsrc(c)),
                      waits=[e1, e2, xb_free2.get(c - 4)], inc=lnX[c % 4])
            xn_c = AR1[:, c * 512:(c + 1) * 512]
            emit("vector", lambda v, c=c, xn_c=xn_c: v.tensor_tensor(xn_c, xb[c % 4], meanb,
                                                                     ALU.subtract),
                 waits=[xs])
            fs = emit("vector", lambda v, c=c, xn_c=xn_c: v.tensor_tensor(xn_c, xn_c, rstdb,
                                                                          ALU.mult), inc=cV)
            xb_free2[c] = fs
        return at(cV)

    def xn_chunk(c):
        return AR1[:, c * 512:(c + 1) * 512]

    def ffn_span(xsrc, ydst):
        xn_ready = ln_span(xsrc)
        yacc = AR2
        zm = emit("vector", lambda v: v.memset(yacc[:, 0:16384], 0.0), waits=[xn_ready], inc=cV)
        w_ring = [AR3[:, i * 1024:(i + 1) * 1024] for i in range(4)]
        w2_ring = [AR3[:, (4 + i) * 1024:(5 + i) * 1024] for i in range(4)]
        gs = [AR4[:, nb * 512:(nb + 1) * 512] for nb in range(8)]
        wfree, w2free = {}, {}
        bank_free = [None] * 8
        for grp in range(14):
            for mat in range(2):
                wsrc_d = wdq1 if mat == 0 else wdq3
                for kb in range(32):
                    i = grp * 64 + mat * 32 + kb
                    wsn = emit("sync", lambda s, kb=kb, grp=grp, wsrc_d=wsrc_d, sl=i % 4: s.dma_start(
                        out=w_ring[sl],
                        in_=wsrc_d[kb * 128:(kb + 1) * 128, grp * 1024:(grp + 1) * 1024]),
                        waits=[xn_ready, wfree.get(i - 4)], inc=fw4[i % 4])
                    for nb in range(8):
                        wts = [wsn]
                        if kb == 0 and bank_free[nb] is not None:
                            wts.append(bank_free[nb])
                        sn = emit("tensor", lambda t, kb=kb, nb=nb, sl=i % 4: t.matmul(
                            bank(nb), w_ring[sl][:, nb * 128:(nb + 1) * 128], xn_chunk(kb),
                            start=(kb == 0), stop=(kb == 31)),
                            waits=wts, inc=(cP if (nb == 7 or kb == 31) else None))
                        if kb == 31:
                            if mat == 0:
                                bank_free[nb] = emit(
                                    "scalar", lambda s, nb=nb: s.activation(gs[nb], bank(nb), AF.Silu),
                                    waits=[sn], inc=cA)
                            else:
                                bank_free[nb] = emit(
                                    "vector", lambda v, nb=nb: v.tensor_tensor(gs[nb], gs[nb], bank(nb), ALU.mult),
                                    waits=[sn], inc=cV)
                    wfree[i] = at(cP)
            h_ready = at(cV)
            for ygrp in range(4):
                for kb8 in range(8):
                    i = grp * 32 + ygrp * 8 + kb8
                    wsn = emit("sync", lambda s, grp=grp, ygrp=ygrp, kb8=kb8, sl=i % 4: s.dma_start(
                        out=w2_ring[sl],
                        in_=wdq2[(grp * 8 + kb8) * 128:(grp * 8 + kb8 + 1) * 128,
                                 ygrp * 1024:(ygrp + 1) * 1024]),
                        waits=[h_ready, w2free.get(i - 4)], inc=fw24[i % 4])
                    for yc in range(8):
                        wts = [wsn]
                        if kb8 == 0 and bank_free[yc] is not None:
                            wts.append(bank_free[yc])
                        sn = emit("tensor", lambda t, kb8=kb8, yc=yc, sl=i % 4: t.matmul(
                            bank(yc), w2_ring[sl][:, yc * 128:(yc + 1) * 128], gs[kb8],
                            start=(kb8 == 0), stop=(kb8 == 7)),
                            waits=wts, inc=(cP if (yc == 7 or kb8 == 7) else None))
                        if kb8 == 7:
                            yo = (ygrp * 8 + yc) * 512
                            bank_free[yc] = emit("vector", lambda v, yo=yo, yc=yc: v.scalar_tensor_tensor(
                                yacc[:, yo:yo + 512], bank(yc), 1.0, yacc[:, yo:yo + 512],
                                ALU.mult, ALU.add),
                                waits=[sn, zm], inc=cV)
                    w2free[i] = at(cP)
        yacc_done = at(cV)
        xb = [AR4[:, i * 512:(i + 1) * 512] for i in range(2)]
        ob = [AR4[:, (2 + i) * 512:(3 + i) * 512] for i in range(2)]
        xfree, ofree = {}, {}
        for c in range(NCH):
            xs = emit("gpsimd", lambda g, c=c: g.dma_start(out=xb[c % 2], in_=xsrc(c)),
                      waits=[yacc_done, xfree.get(c - 2)], inc=resX[c % 2])
            vsn = emit("vector", lambda v, c=c: v.scalar_tensor_tensor(
                ob[c % 2], yacc[:, c * 512:(c + 1) * 512], 0.5, xb[c % 2], ALU.mult, ALU.add),
                waits=[xs, ofree.get(c - 2)], inc=cV)
            xfree[c] = vsn
            ofree[c] = emit("gpsimd", lambda g, c=c: g.dma_start(out=ydst(c), in_=ob[c % 2]),
                            waits=[vsn], inc=resY[c % 2])

    for sp in range(S // TSP):
        t0 = sp * TSP
        ffn_span(lambda c, t0=t0: x_fm[c * 128:(c + 1) * 128, t0:t0 + TSP],
                 lambda c, t0=t0: x1[c * 128:(c + 1) * 128, t0:t0 + TSP])
        barrier()

    for sp in range(S // TSP):
        t0 = sp * TSP
        xn_ready = ln_span(lambda c, t0=t0: x1[c * 128:(c + 1) * 128, t0:t0 + TSP])
        qkvbuf = [AR3[:, i * 768:(i + 1) * 768] for i in range(4)]
        evb = [AR4[:, i * 512:(i + 1) * 512] for i in range(2)]
        wfree, evfree = {}, {}
        bank_free = [None] * 6
        for wave in range(2):
            for kb in range(32):
                i = wave * 32 + kb
                wsn = emit("sync", lambda s, kb=kb, wave=wave, sl=i % 4: s.dma_start(
                    out=qkvbuf[sl], in_=wqkvT[kb * 128:(kb + 1) * 128, wave * 768:(wave + 1) * 768]),
                    waits=[xn_ready, wfree.get(i - 4)], inc=fw4[i % 4])
                for nb in range(6):
                    wts = [wsn]
                    if kb == 0 and bank_free[nb] is not None:
                        wts.append(bank_free[nb])
                    sn = emit("tensor", lambda t, kb=kb, nb=nb, sl=i % 4: t.matmul(
                        bank(nb), qkvbuf[sl][:, nb * 128:(nb + 1) * 128], xn_chunk(kb),
                        start=(kb == 0), stop=(kb == 31)),
                        waits=wts, inc=(cP if (nb == 5 or kb == 31) else None))
                    if kb == 31:
                        j = wave * 6 + nb
                        ev = emit("vector", lambda v, nb=nb, j=j: v.tensor_copy(evb[j % 2], bank(nb)),
                                  waits=[sn, evfree.get(j - 2)], inc=cV)
                        bank_free[nb] = ev
                        evfree[j] = emit("gpsimd", lambda g, j=j, t0=t0: g.dma_start(
                            out=qkp[j * 128:(j + 1) * 128, dcol(t0):dcol(t0) + TSP], in_=evb[j % 2]),
                            waits=[ev], inc=resY[j % 2])
                wfree[i] = at(cP)
        barrier()

    hb = AR4[:, 0:8]
    for c in range(QKV // 128):
        s1 = emit("gpsimd", lambda g, c=c: g.dma_start(
            out=hb[:, 0:4], in_=qkp[c * 128:(c + 1) * 128, 8 + 2 * OWN - 4:8 + 2 * OWN]),
            inc=resX[0])
        s2 = emit("gpsimd", lambda g, c=c: g.dma_start(
            out=hb[:, 4:8], in_=qkp[c * 128:(c + 1) * 128, OWN:4 + OWN]), inc=resX[1])
        v1 = emit("vector", lambda v: v.tensor_scalar(hb[:, 0:4], hb[:, 0:4], ioddc, None, ALU.mult),
                  waits=[s1], inc=cV)
        v2 = emit("vector", lambda v: v.tensor_scalar(hb[:, 4:8], hb[:, 4:8], ievenc, None, ALU.mult),
                  waits=[s2], inc=cV)
        st1 = emit("gpsimd", lambda g, c=c: g.dma_start(out=qkp[c * 128:(c + 1) * 128, 0:4], in_=hb[:, 0:4]),
                   waits=[v1], inc=resY[0])
        emit("gpsimd", lambda g, c=c: g.dma_start(out=qkp[c * 128:(c + 1) * 128, 4 + OWN:8 + OWN],
                                                  in_=hb[:, 4:8]),
             waits=[v2], inc=resY[1])
        barrier()

    def load_tiles(dst, srcs):
        for idx, s_ in enumerate(srcs):
            emit("gpsimd", lambda g, idx=idx, s_=s_: g.dma_start(
                out=dst[:, idx * 128:(idx + 1) * 128], in_=s_), inc=dCT)
        return at(dCT)

    kc_srcs = [kc1L[dt, oc, src] for dt in range(3) for oc in range(2) for src in range(2)] + \
              [kc2L[dt, oc, src] for dt in range(3) for oc in range(2) for src in range(2)]
    kct = AR3[:, 0:24 * 128]
    kc_ready = load_tiles(kct, kc_srcs)

    def kc1_t(dt, oc, src):
        return kct[:, (dt * 4 + oc * 2 + src) * 128:(dt * 4 + oc * 2 + src + 1) * 128]

    def kc2_t(dt, oc, src):
        return kct[:, (12 + dt * 4 + oc * 2 + src) * 128:(13 + dt * 4 + oc * 2 + src) * 128]

    for slot in range(64):
        emit("vector", lambda v, slot=slot: v.memset(
            AR2[:, 2048 + slot * 65 + 64:2048 + slot * 65 + 65], 1.0))
    emit("vector", lambda v: v.memset(AR2[:, 0:1], 0.0), inc=cV)
    vone_ones = at(cV)

    for sp in range(S // TSP):
        t0 = sp * TSP
        dc0 = dcol(t0)
        rd0 = dc0 - 4
        kin = [AR1[:, i * 520:i * 520 + 516] for i in range(2)]
        k1o = [AR1[:, 1040 + i * 520:1040 + i * 520 + 514] for i in range(2)]
        k2o = [AR1[:, 2080 + i * 512:2080 + (i + 1) * 512] for i in range(2)]
        ksq = [AR1[:, 3104 + i * 512:3104 + (i + 1) * 512] for i in range(2)]
        qpb = [AR1[:, 4128 + i * 512:4128 + (i + 1) * 512] for i in range(4)]
        qm = AR1[0:64, 6176:6688]
        rnk = AR1[0:4, 6688:7200]
        kro = [AR1[:, 7200 + i * 512:7200 + (i + 1) * 512] for i in range(2)]
        vtile = [AR1[:, 8224 + i * 512:8224 + (i + 1) * 512] for i in range(2)]
        kins = []
        for ch in range(2):
            kins.append(emit("gpsimd", lambda g, ch=ch, rd0=rd0: g.dma_start(
                out=kin[ch], in_=qkp[LAT + ch * 128:LAT + (ch + 1) * 128, rd0:rd0 + 516]),
                waits=[kc_ready], inc=resX[ch]))
        msnap = None
        for c in range(8):
            s_ = emit("gpsimd", lambda g, c=c, dc0=dc0: g.dma_start(
                out=qpb[c % 4], in_=qkp[c * 128:(c + 1) * 128, dc0:dc0 + TSP]),
                waits=[msnap if c >= 4 else None], inc=qp4[c % 4])
            msnap = emit("tensor", lambda t, c=c: t.matmul(bank(0, 0, 64), mq_t(c), qpb[c % 4],
                                                           start=(c == 0), stop=(c == 7)),
                         waits=[s_], inc=cP)
        qme = emit("vector", lambda v: v.tensor_copy(qm, bank(0, 0, 64)), waits=[msnap], inc=cV)
        prev_evac = None
        for oc in range(2):
            for (go, gw, bk) in ((0, 512, 1), (512, 2, 2)):
                last = None
                for src in range(2):
                    for dt in range(3):
                        st_ = (src == 0 and dt == 0)
                        sp_ = (src == 1 and dt == 2)
                        wts = [kins[src]]
                        if st_ and prev_evac is not None:
                            wts.append(prev_evac)
                        last = emit("tensor",
                                    lambda t, oc=oc, src=src, dt=dt, go=go, gw=gw, bk=bk, st_=st_, sp_=sp_:
                                    t.matmul(bank(bk, 0, 128, 0, gw), kc1_t(dt, oc, src),
                                             kin[src][:, go + dt:go + dt + gw], start=st_, stop=sp_),
                                    waits=wts, inc=(cP if sp_ else None))
                prev_evac = emit("vector", lambda v, oc=oc, go=go, gw=gw, bk=bk: v.tensor_scalar(
                    k1o[oc][:, go:go + gw], bank(bk, 0, 128, 0, gw), kb_col(oc), None, ALU.add),
                    waits=[last], inc=cV)
        if sp == 0 or sp == 2:
            gatec = ioddc if sp == 0 else ievenc
            for oc in range(2):
                emit("vector", lambda v, oc=oc, gatec=gatec: v.tensor_scalar(
                    k1o[oc][:, 0:2], k1o[oc][:, 0:2], gatec, None, ALU.mult), inc=cV)
        k1_done = at(cV)
        for oc in range(2):
            last = None
            for src in range(2):
                for dt in range(3):
                    st_ = (src == 0 and dt == 0)
                    last = emit("tensor", lambda t, oc=oc, src=src, dt=dt, st_=st_: t.matmul(
                        bank(3 + oc), kc2_t(dt, oc, src), k1o[src][:, dt:dt + 512],
                        start=st_, stop=False),
                        waits=[k1_done])
            last = emit("tensor", lambda t, oc=oc: t.matmul(bank(3 + oc), s5k_t(oc), qm,
                                                            start=False, stop=True),
                        waits=[qme], inc=cP)
            emit("vector", lambda v, oc=oc: v.tensor_scalar(k2o[oc], bank(3 + oc), kb_col(2 + oc),
                                                            None, ALU.add),
                 waits=[last], inc=cV)
        k2_done = at(cV)
        sq1 = emit("scalar", lambda s: s.activation(ksq[0], k2o[0], AF.Square), waits=[k2_done], inc=cA)
        sq2 = emit("scalar", lambda s: s.activation(ksq[1], k2o[1], AF.Square), inc=cA)
        emit("tensor", lambda t: t.matmul(bank(5, 0, 4), ok_t(0), ksq[0], start=True, stop=False),
             waits=[sq1])
        o2 = emit("tensor", lambda t: t.matmul(bank(5, 0, 4), ok_t(1), ksq[1], start=False, stop=True),
                  waits=[sq2], inc=cP)
        s1_ = emit("scalar", lambda s: s.activation(rnk, bank(5, 0, 4), AF.Sqrt), waits=[o2], inc=cA)
        emit("vector", lambda v: v.tensor_scalar_max(rnk, rnk, 1e-12), waits=[s1_])
        emit("vector", lambda v: v.reciprocal(rnk, rnk))
        v3_ = emit("vector", lambda v: v.tensor_scalar(rnk, rnk, ktc[0:4, 0:1], 0.125,
                                                       ALU.mult, ALU.mult), inc=cV)
        prev_tt = None
        for ch in range(2):
            bb = emit("tensor", lambda t, ch=ch: t.matmul(bank(6), bk_t(ch), rnk, start=True, stop=True),
                      waits=[v3_] + ([prev_tt] if prev_tt else []), inc=cP)
            prev_tt = emit("vector", lambda v, ch=ch: v.tensor_tensor(k2o[ch], k2o[ch], bank(6), ALU.mult),
                           waits=[bb], inc=cV)
        kh_done = at(cV)
        cksb = AR2[:, 0:512]
        sksb = AR2[:, 512:1024]
        tmp1 = AR2[:, 1024:1536]
        tmp2 = AR2[:, 1536:2048]
        c1 = emit("gpsimd", lambda g, t0=t0: g.dma_start(out=cksb, in_=cosk[:, t0:t0 + TSP]),
                  inc=dcs[0])
        c2 = emit("gpsimd", lambda g, t0=t0: g.dma_start(out=sksb, in_=sink[:, t0:t0 + TSP]),
                  inc=dcs[1])
        emit("vector", lambda v: v.tensor_tensor(tmp1, k2o[0], cksb, ALU.mult), waits=[kh_done, c1, c2])
        emit("vector", lambda v: v.tensor_tensor(tmp2, k2o[1], sksb, ALU.mult))
        emit("vector", lambda v: v.tensor_tensor(kro[0], tmp1, tmp2, ALU.subtract))
        emit("vector", lambda v: v.tensor_tensor(tmp1, k2o[0], sksb, ALU.mult))
        emit("vector", lambda v: v.tensor_tensor(tmp2, k2o[1], cksb, ALU.mult))
        rp = emit("vector", lambda v: v.tensor_tensor(kro[1], tmp1, tmp2, ALU.add), inc=cV)
        for ch in range(2):
            emit("gpsimd", lambda g, ch=ch, t0=t0: g.dma_start(
                out=kfin[ch * 128:(ch + 1) * 128, t0:t0 + TSP], in_=kro[ch]), waits=[rp], inc=cX)
        vs1 = emit("gpsimd", lambda g, dc0=dc0: g.dma_start(
            out=vtile[0], in_=qkp[LAT + KVD:LAT + KVD + 128, dc0:dc0 + TSP]), inc=dvt[0])
        vs2 = emit("gpsimd", lambda g, rd0=rd0: g.dma_start(
            out=vtile[1], in_=qkp[LAT + KVD + 128:LAT + KVD + 256, rd0 + 3:rd0 + 3 + TSP]),
            inc=dvt[1])
        prev_cp = None
        for vch in range(2):
            for q4 in range(4):
                tci = sp * 4 + q4
                tr = emit("tensor", lambda t, vch=vch, q4=q4: t.transpose(
                    bank(7, 0, 128, 0, 128), vtile[vch][:, q4 * 128:(q4 + 1) * 128], identt),
                    waits=[vs1, vs2, vone_ones] + ([prev_cp] if prev_cp else []), inc=cP)
                for j in range(2):
                    slot = tci * 4 + vch * 2 + j
                    prev_cp = emit("vector", lambda v, j=j, slot=slot: v.tensor_copy(
                        AR2[:, 2048 + slot * 65:2048 + slot * 65 + 64],
                        bank(7, 0, 128, j * 64, (j + 1) * 64)),
                        waits=[tr], inc=cV)
        barrier()

    qc_srcs = [qc1L[dt, oc, src] for dt in range(3) for oc in range(8) for src in range(2)]
    qct = AR3[:, 0:48 * 128]
    qc_ready = load_tiles(qct, qc_srcs)

    def qc1_t(dt, oc, src):
        return qct[:, (dt * 16 + oc * 2 + src) * 128:(dt * 16 + oc * 2 + src + 1) * 128]

    q2ring = AR2[:, 6208:6208 + 24 * 128]

    for sp in range(OWN // TSP):
        t0 = sp * TSP
        dc0 = dcol(t0)
        rd0 = dc0 - 4
        qin = [AR1[:, i * 520:i * 520 + 516] for i in range(8)]
        q1o = [AR1[:, 4160 + i * 520:4160 + i * 520 + 514] for i in range(8)]
        q2o = [AR1[:, 8320 + i * 512:8320 + (i + 1) * 512] for i in range(8)]
        km = AR1[0:64, 12416:12928]
        rnq = AR1[0:16, 12928:13440]
        kpb = [AR1[:, 13440 + i * 512:13440 + (i + 1) * 512] for i in range(2)]
        qro = [AR1[:, 14464 + i * 512:14464 + (i + 1) * 512] for i in range(2)]
        qsq = [AR2[:, 9280 + i * 512:9280 + (i + 1) * 512] for i in range(2)]
        qins = []
        for c in range(8):
            qins.append(emit("gpsimd", lambda g, c=c, rd0=rd0: g.dma_start(
                out=qin[c], in_=qkp[c * 128:(c + 1) * 128, rd0:rd0 + 516]),
                waits=[qc_ready], inc=qi8[c]))
        m_ = None
        for c in range(2):
            s_ = emit("gpsimd", lambda g, c=c, dc0=dc0: g.dma_start(
                out=kpb[c], in_=qkp[LAT + c * 128:LAT + (c + 1) * 128, dc0:dc0 + TSP]),
                inc=kp2[c])
            m_ = emit("tensor", lambda t, c=c: t.matmul(bank(0, 0, 64), mk_t(c), kpb[c],
                                                        start=(c == 0), stop=(c == 1)),
                      waits=[s_], inc=cP)
        kme = emit("vector", lambda v: v.tensor_copy(km, bank(0, 0, 64)), waits=[m_], inc=cV)
        prev_evac = None
        for oc in range(8):
            for (go, gw, bk) in ((0, 512, 1), (512, 2, 2)):
                last = None
                for src in range(2):
                    ic = (oc % 4) + src * 4
                    for dt in range(3):
                        st_ = (src == 0 and dt == 0)
                        sp_ = (src == 1 and dt == 2)
                        wts = [qins[ic]]
                        if st_ and prev_evac is not None:
                            wts.append(prev_evac)
                        last = emit("tensor",
                                    lambda t, oc=oc, ic=ic, src=src, dt=dt, go=go, gw=gw, bk=bk, st_=st_, sp_=sp_:
                                    t.matmul(bank(bk, 0, 128, 0, gw), qc1_t(dt, oc, src),
                                             qin[ic][:, go + dt:go + dt + gw], start=st_, stop=sp_),
                                    waits=wts, inc=(cP if sp_ else None))
                prev_evac = emit("vector", lambda v, oc=oc, go=go, gw=gw, bk=bk: v.tensor_scalar(
                    q1o[oc][:, go:go + gw], bank(bk, 0, 128, 0, gw), qb_col(oc), None, ALU.add),
                    waits=[last], inc=cV)
        if sp == 0:
            for oc in range(8):
                emit("vector", lambda v, oc=oc: v.tensor_scalar(
                    q1o[oc][:, 0:2], q1o[oc][:, 0:2], ioddc, None, ALU.mult), inc=cV)
        q1_done = at(cV)
        prev_evac2 = None
        for oc in range(8):
            tl = [(dt, ic) for ic in range(8) for dt in range(3)]
            for idx, (dt, ic) in enumerate(tl):
                emit("gpsimd", lambda g, idx=idx, dt=dt, ic=ic, oc=oc: g.dma_start(
                    out=q2ring[:, idx * 128:(idx + 1) * 128], in_=qc2L[dt, oc, ic]),
                    waits=[prev_evac2] if idx == 0 else [], inc=dQR)
            lsn_all = at(dQR)
            last = None
            for idx, (dt, ic) in enumerate(tl):
                st_ = (idx == 0)
                wts = [lsn_all, q1_done]
                if st_ and prev_evac2 is not None:
                    wts.append(prev_evac2)
                last = emit("tensor", lambda t, idx=idx, dt=dt, ic=ic, st_=st_: t.matmul(
                    bank(3), q2ring[:, idx * 128:(idx + 1) * 128], q1o[ic][:, dt:dt + 512],
                    start=st_, stop=False),
                    waits=wts)
            last = emit("tensor", lambda t, oc=oc: t.matmul(bank(3), s5q_t(oc), km,
                                                            start=False, stop=True),
                        waits=[kme], inc=cP)
            prev_evac2 = emit("vector", lambda v, oc=oc: v.tensor_scalar(
                q2o[oc], bank(3), qb_col(8 + oc), None, ALU.add),
                waits=[last], inc=cV)
        q2_done = at(cV)
        osn = {}
        m_ = None
        for c in range(8):
            sq = emit("scalar", lambda s, c=c: s.activation(qsq[c % 2], q2o[c], AF.Square),
                      waits=[q2_done, osn.get(c - 2)], inc=cA)
            m_ = emit("tensor", lambda t, c=c: t.matmul(bank(5, 0, 16), oq_t(c), qsq[c % 2],
                                                        start=(c == 0), stop=(c == 7)),
                      waits=[sq], inc=cP)
            osn[c] = m_
        s1_ = emit("scalar", lambda s: s.activation(rnq, bank(5, 0, 16), AF.Sqrt), waits=[m_], inc=cA)
        emit("vector", lambda v: v.tensor_scalar_max(rnq, rnq, 1e-12), waits=[s1_])
        v3_ = emit("vector", lambda v: v.reciprocal(rnq, rnq), inc=cV)
        prev_tt = None
        for ch in range(8):
            bb = emit("tensor", lambda t, ch=ch: t.matmul(bank(6), bq_t(ch), rnq, start=True, stop=True),
                      waits=[v3_] + ([prev_tt] if prev_tt else []), inc=cP)
            prev_tt = emit("vector", lambda v, ch=ch: v.tensor_tensor(q2o[ch], q2o[ch], bank(6), ALU.mult),
                           waits=[bb], inc=cV)
        qh_done = at(cV)
        cksb = AR2[:, 9280 + 1024:9280 + 1536]
        sksb = AR2[:, 9280 + 1536:9280 + 2048]
        tmp1 = AR2[:, 9280 + 2048:9280 + 2560]
        tmp2 = AR2[:, 9280 + 2560:9280 + 3072]
        c1 = emit("gpsimd", lambda g, t0=t0: g.dma_start(out=cksb, in_=cosk[:, t0:t0 + TSP]),
                  inc=dcs[0])
        c2 = emit("gpsimd", lambda g, t0=t0: g.dma_start(out=sksb, in_=sink[:, t0:t0 + TSP]),
                  inc=dcs[1])
        prev_st = None
        for pc in range(4):
            e, o = q2o[pc], q2o[pc + 4]
            emit("vector", lambda v, e=e: v.tensor_tensor(tmp1, e, cksb, ALU.mult),
                 waits=[qh_done, c1, c2] + ([prev_st] if prev_st else []))
            emit("vector", lambda v, o=o: v.tensor_tensor(tmp2, o, sksb, ALU.mult))
            emit("vector", lambda v: v.tensor_tensor(qro[0], tmp1, tmp2, ALU.subtract))
            emit("vector", lambda v, e=e: v.tensor_tensor(tmp1, e, sksb, ALU.mult))
            emit("vector", lambda v, o=o: v.tensor_tensor(tmp2, o, cksb, ALU.mult))
            rp = emit("vector", lambda v: v.tensor_tensor(qro[1], tmp1, tmp2, ALU.add), inc=cV)
            emit("gpsimd", lambda g, pc=pc, t0=t0: g.dma_start(
                out=qfin[pc * 128:(pc + 1) * 128, t0:t0 + TSP], in_=qro[0]), waits=[rp], inc=cX)
            prev_st = emit("gpsimd", lambda g, pc=pc, t0=t0: g.dma_start(
                out=qfin[(4 + pc) * 128:(5 + pc) * 128, t0:t0 + TSP], in_=qro[1]),
                waits=[rp], inc=cX)
        barrier()

    for kvh in range(4):
        emit("gpsimd", lambda g, kvh=kvh: g.dma_start(
            out=AR1[0:32, kvh * 2048:(kvh + 1) * 2048], in_=kfin[32 * kvh:32 * kvh + 32, :]),
            inc=dKS)
        emit("gpsimd", lambda g, kvh=kvh: g.dma_start(
            out=AR1[32:64, kvh * 2048:(kvh + 1) * 2048],
            in_=kfin[128 + 32 * kvh:128 + 32 * kvh + 32, :]), inc=dKS)
    kst_ready = at(dKS)
    aout = AR2[0:64, 6208:6208 + 16 * 512]
    for sp in range(OWN // TSP):
        t0 = sp * TSP
        qst = [AR1[:, 8192 + i * 512:8192 + (i + 1) * 512] for i in range(2)]
        expb = [AR1[:, 9216 + i * 512:9216 + (i + 1) * 512] for i in range(2)]
        maskb = [AR1[:, 10240 + i * 512:10240 + (i + 1) * 512] for i in range(2)]
        avs = AR1[0:64, 11264:11776]
        x1b = [AR1[:, 11776 + i * 512:11776 + (i + 1) * 512] for i in range(2)]
        x2b = [AR1[:, 12800 + i * 512:12800 + (i + 1) * 512] for i in range(2)]
        qfree = {}
        mask_done = {}
        exp_done = {}
        avlast = None
        zblast = None
        for h in range(NH):
            kv = h // 4
            gi0 = h * 16
            emit("gpsimd", lambda g, h=h, t0=t0: g.dma_start(
                out=qst[h % 2][0:32, :], in_=qfin[32 * h:32 * h + 32, t0:t0 + TSP]),
                waits=[kst_ready, qfree.get(h - 2)], inc=qs2[h % 2])
            emit("gpsimd", lambda g, h=h, t0=t0: g.dma_start(
                out=qst[h % 2][32:64, :], in_=qfin[512 + 32 * h:512 + 32 * h + 32, t0:t0 + TSP]),
                inc=qs2[h % 2])
            qrdy = at(qs2[h % 2])
            for ci in range(16):
                gi = gi0 + ci
                sb_ = ci % 2
                msn = emit("gpsimd", lambda g, ci=ci, sb_=sb_, t0=t0: g.dma_start(
                    out=maskb[sb_], in_=maskin[ci][:, t0:t0 + TSP]),
                    waits=[mask_done.get(gi - 2)], inc=mk2[sb_])
                sc = emit("tensor", lambda t, ci=ci, kv=kv, sb_=sb_, h=h: t.matmul(
                    bank(sb_), AR1[0:64, kv * 2048 + ci * 128:kv * 2048 + (ci + 1) * 128],
                    qst[h % 2][0:64, :], start=True, stop=True, tile_position=(0, 0)),
                    waits=[qrdy, exp_done.get(gi - 2)], inc=cP)
                ex = emit("scalar", lambda s, sb_=sb_: s.activation(expb[sb_], bank(sb_), AF.Exp,
                                                                    bias=negM),
                          waits=[sc], inc=cA)
                exp_done[gi] = ex
                mm_ = emit("vector", lambda v, sb_=sb_: v.tensor_tensor(expb[sb_], expb[sb_],
                                                                        maskb[sb_], ALU.mult),
                           waits=[ex, msn], inc=cV)
                mask_done[gi] = mm_
                slot = ci * 4 + kv
                avw = [mm_]
                if ci == 0 and avlast is not None:
                    avw.append(avlast)
                av = emit("tensor", lambda t, slot=slot, sb_=sb_, ci=ci: t.matmul(
                    bank(3, 0, 65), AR2[:, 2048 + slot * 65:2048 + (slot + 1) * 65], expb[sb_],
                    start=(ci == 0), stop=(ci == 15)),
                    waits=avw, inc=(cP if ci == 15 else None))
                if ci == 15:
                    av_done = av
            qfree[h] = at(cP)
            zr = PRow[0:1, 3072:3584]
            rz = emit("vector", lambda v: v.reciprocal(zr, bank(3, 64, 65)), waits=[av_done], inc=cV)
            zb = emit("tensor", lambda t: t.matmul(bank(4, 0, 64), ones1r[0:1, 0:64], zr,
                                                   start=True, stop=True),
                      waits=[rz] + ([zblast] if zblast else []), inc=cP)
            ae = emit("scalar", lambda s: s.copy(avs, bank(3, 0, 64)), waits=[rz], inc=cA)
            avlast = ae
            zblast = emit("vector", lambda v, h=h: v.tensor_tensor(
                aout[:, h * 512:(h + 1) * 512], avs, bank(4, 0, 64), ALU.mult),
                waits=[zb, ae], inc=cV)
        aout_done = at(cV)
        wob = [AR3[:, i * 768:(i + 1) * 768] for i in range(4)]
        wofree = {}
        bank_free2 = [None] * 6
        xfree, x2free = {}, {}
        for wave in range(6):
            ncb = 6 if wave < 5 else 2
            for h in range(NH):
                i = wave * 16 + h
                wsn = emit("sync", lambda s, wave=wave, h=h, sl=i % 4, ncb=ncb: s.dma_start(
                    out=wob[sl][0:64, 0:ncb * 128],
                    in_=woT[64 * h:64 * h + 64, wave * 768:wave * 768 + ncb * 128]),
                    waits=[aout_done, wofree.get(i - 4)], inc=fw4[i % 4])
                for nb in range(ncb):
                    wts = [wsn]
                    if h == 0 and bank_free2[nb] is not None:
                        wts.append(bank_free2[nb])
                    sn = emit("tensor", lambda t, h=h, nb=nb, sl=i % 4: t.matmul(
                        bank(nb), wob[sl][0:64, nb * 128:(nb + 1) * 128],
                        aout[:, h * 512:(h + 1) * 512],
                        start=(h == 0), stop=(h == NH - 1), tile_position=(0, 0)),
                        waits=wts, inc=(cP if (nb == ncb - 1 or h == NH - 1) else None))
                    if h == NH - 1:
                        c = wave * 6 + nb
                        xs = emit("gpsimd", lambda g, c=c, t0=t0: g.dma_start(
                            out=x1b[c % 2], in_=x1[c * 128:(c + 1) * 128, t0:t0 + TSP]),
                            waits=[xfree.get(c - 2)], inc=resX[c % 2])
                        vv = emit("vector", lambda v, c=c, nb=nb: v.scalar_tensor_tensor(
                            x2b[c % 2], bank(nb), 1.0, x1b[c % 2], ALU.mult, ALU.add),
                            waits=[sn, xs, x2free.get(c - 2)], inc=cV)
                        bank_free2[nb] = vv
                        xfree[c] = vv
                        x2free[c] = emit("gpsimd", lambda g, c=c, t0=t0: g.dma_start(
                            out=x2[c * 128:(c + 1) * 128, t0:t0 + TSP], in_=x2b[c % 2]),
                            waits=[vv], inc=st2[c % 2])
                wofree[i] = at(cP)
        barrier()

    for sp in range(OWN // TSP):
        t0 = sp * TSP
        ffn_span(lambda c, t0=t0: x2[c * 128:(c + 1) * 128, t0:t0 + TSP],
                 lambda c, t0=t0: y_out[c * 128:(c + 1) * 128, t0:t0 + TSP])
        barrier()

    emit("gpsimd", None, waits=[at(c) for c in DCHS])

    with nc.Block() as block:
        for ename in ENGINES:
            def make(ename):
                def run(eng):
                    for fn, waits, incspec in P.ops[ename]:
                        for sem, val in waits:
                            eng.wait_ge(sem, val)
                        if fn is None:
                            continue
                        instr = fn(eng)
                        if incspec is not None:
                            instr.then_inc(*incspec)
                return run
            getattr(block, ename)(make(ename))
    est.close()
    return nc



def _deint_perm(nheads):
    n = nheads * HD
    perm = np.zeros(n, dtype=np.int64)
    half = n // 2
    for h in range(nheads):
        for i in range(32):
            perm[h * 32 + i] = h * HD + 2 * i
            perm[half + h * 32 + i] = h * HD + 2 * i + 1
    return perm


def prep_inputs(inputs):
    f = lambda k: np.asarray(inputs[k], dtype=np.float32)
    x = f("L0_attn_x")
    wqkv = f("L0_attn_w_qkv")
    wo = f("L0_attn_w_o")
    qc1w, qc1b = f("L0_attn_qc1_w"), f("L0_attn_qc1_b")
    qc2w, qc2b = f("L0_attn_qc2_w"), f("L0_attn_qc2_b")
    kc1w, kc1b = f("L0_attn_kc1_w"), f("L0_attn_kc1_b")
    kc2w, kc2b = f("L0_attn_kc2_w"), f("L0_attn_kc2_b")
    ktemp = f("L0_attn_key_temp")
    w1, w1s = f("L0_ffn_w1"), f("L0_ffn_w1_scale")
    w3, w3s = f("L0_ffn_w3"), f("L0_ffn_w3_scale")
    w2, w2s = f("L0_ffn_w2"), f("L0_ffn_w2_scale")

    qperm = _deint_perm(NH)
    kperm = _deint_perm(NKV)
    rows = np.concatenate([qperm, LAT + kperm, np.arange(LAT + KVD, QKV)])
    wqkvT = np.ascontiguousarray(wqkv[rows, :].T)
    woT = np.ascontiguousarray(wo.T)
    w1T = np.ascontiguousarray(w1.T)
    w3T = np.ascontiguousarray(w3.T)
    w2T = np.ascontiguousarray(w2.T)
    s1row = np.ascontiguousarray(np.repeat(w1s, 128, axis=0).T)
    s3row = np.ascontiguousarray(np.repeat(w3s, 128, axis=0).T)
    s2row = np.ascontiguousarray(np.repeat(w2s, 128, axis=0).T)

    def conv1_tiles(w, b, nheads, perm):
        C = nheads * HD
        nch = C // 128
        nch_half = (C // 2) // 128
        dense = np.zeros((C, C, 3), dtype=np.float32)
        for o in range(C):
            g = o // HD
            dense[o, g * HD:(g + 1) * HD, :] = w[o]
        dd = dense[perm][:, perm]
        tiles = np.zeros((3, nch, 2, 128, 128), dtype=np.float32)
        for dt in range(3):
            for oc in range(nch):
                for src in range(2):
                    ic = (oc % nch_half) + src * nch_half
                    tiles[dt, oc, src] = dd[oc * 128:(oc + 1) * 128,
                                            ic * 128:(ic + 1) * 128, dt].T
        return np.ascontiguousarray(tiles), np.ascontiguousarray(b[perm])

    def conv2_tiles(w, b, C, perm):
        nch = C // 128
        dd = w[perm][:, perm]
        tiles = np.zeros((3, nch, nch, 128, 128), dtype=np.float32)
        for dt in range(3):
            for oc in range(nch):
                for ic in range(nch):
                    tiles[dt, oc, ic] = dd[oc * 128:(oc + 1) * 128,
                                           ic * 128:(ic + 1) * 128, dt].T
        return np.ascontiguousarray(tiles), np.ascontiguousarray(b[perm])

    qc1L, qb1 = conv1_tiles(qc1w, qc1b, NH, qperm)
    qc2L, qb2 = conv2_tiles(qc2w, qc2b, LAT, qperm)
    kc1L, kb1 = conv1_tiles(kc1w, kc1b, NKV, kperm)
    kc2L, kb2 = conv2_tiles(kc2w, kc2b, KVD, kperm)
    qbias = np.zeros((128, 16), dtype=np.float32)
    for c in range(8):
        qbias[:, c] = qb1[c * 128:(c + 1) * 128]
        qbias[:, 8 + c] = qb2[c * 128:(c + 1) * 128]
    kbias = np.zeros((128, 4), dtype=np.float32)
    for c in range(2):
        kbias[:, c] = kb1[c * 128:(c + 1) * 128]
        kbias[:, 2 + c] = kb2[c * 128:(c + 1) * 128]

    Mq = np.zeros((8, 128, 64), dtype=np.float32)
    Mk = np.zeros((2, 128, 64), dtype=np.float32)
    S5q = np.zeros((8, 64, 128), dtype=np.float32)
    S5k = np.zeros((2, 64, 128), dtype=np.float32)
    Oq = np.zeros((8, 128, 16), dtype=np.float32)
    Ok = np.zeros((2, 128, 4), dtype=np.float32)
    Bq = np.zeros((8, 16, 128), dtype=np.float32)
    Bk = np.zeros((2, 4, 128), dtype=np.float32)
    for c in range(8):
        for r in range(128):
            i = r % 32
            d0 = 0 if c < 4 else 32
            Mq[c, r, d0 + i] = 1.0 / NH
            S5q[c, d0 + i, r] = 0.5
            h = (c % 4) * 4 + r // 32
            Oq[c, r, h] = 1.0
            Bq[c, h, r] = 1.0
    for c in range(2):
        for r in range(128):
            i = r % 32
            d0 = 0 if c < 1 else 32
            Mk[c, r, d0 + i] = 1.0 / NKV
            S5k[c, d0 + i, r] = 0.5
            h = r // 32
            Ok[c, r, h] = 1.0
            Bk[c, h, r] = 1.0

    ident = np.eye(128, dtype=np.float32)
    shared = dict(wqkvT=wqkvT, woT=woT, w1T=w1T, w3T=w3T, w2T=w2T,
                  s1row=s1row, s3row=s3row, s2row=s2row,
                  qc1L=qc1L, qc2L=qc2L, kc1L=kc1L, kc2L=kc2L,
                  qbias=qbias, kbias=kbias, Mq=Mq, Mk=Mk, S5q=S5q, S5k=S5k,
                  Oq=Oq, Ok=Ok, Bq=Bq, Bk=Bk, ident=ident,
                  ktemp=np.broadcast_to(ktemp.reshape(1, 1), (128, 1)).copy())

    inv = 1.0 / (10000.0 ** (np.arange(0, HD, 2, dtype=np.float32) / HD))
    ang = np.arange(S, dtype=np.float32)[:, None] * inv[None, :]
    cosg = np.cos(ang).T.astype(np.float32)
    sing = np.sin(ang).T.astype(np.float32)

    in_maps = []
    for core in range(8):
        b, h = core // 2, core % 2
        off = h * OWN
        order = np.concatenate([np.arange(off, off + OWN), np.arange(0, off),
                                np.arange(off + OWN, S)])
        gt = order.astype(np.float32)
        m = dict(shared)
        m["x_fm"] = np.ascontiguousarray(x[b].T[:, order])
        m["cosk"] = np.ascontiguousarray(np.tile(cosg[:, order], (4, 1)))
        m["sink"] = np.ascontiguousarray(np.tile(sing[:, order], (4, 1)))
        maskin = np.zeros((16, 128, OWN), dtype=np.float32)
        for ci in range(16):
            maskin[ci] = (gt[ci * 128:(ci + 1) * 128][:, None] <= gt[None, :OWN]).astype(np.float32)
        m["maskin"] = maskin
        m["iodd"] = np.full((128, 1), float(h == 1), dtype=np.float32)
        m["ieven"] = np.full((128, 1), float(h == 0), dtype=np.float32)
        in_maps.append(m)
    return in_maps


def kernel(**inputs):
    in_maps = prep_inputs(inputs)
    nc = build_program()
    res = run_bass_kernel_spmd(nc, in_maps, core_ids=list(range(8)))
    out = np.zeros((4, S, D), dtype=np.float32)
    for core in range(8):
        b, h = core // 2, core % 2
        y = res.results[core]["y"]
        out[b, h * OWN:(h + 1) * OWN, :] = np.asarray(y).T
    return out


# version 03
# speedup vs baseline: 2.3192x; 2.3192x over previous
import numpy as np
from contextlib import ExitStack

import concourse.bass as bass
import concourse.mybir as mybir
from concourse.bass_utils import run_bass_kernel_spmd

F32 = mybir.dt.float32
BF16 = mybir.dt.bfloat16
AF = mybir.ActivationFunctionType
ALU = mybir.AluOpType

D = 4096
INTER = 14336
S = 2048
OWN = 1024
NH, NKV, HD = 16, 4, 64
LAT = NH * HD
KVD = NKV * HD
QKV = LAT + 2 * KVD
TSP = 512
NCH = D // 128

ENGINES = ("sync", "tensor", "vector", "scalar", "gpsimd")


class Ch:
    def __init__(self, sem, step):
        self.sem = sem
        self.step = step
        self.count = 0


class Prog:
    def __init__(self):
        self.ops = {e: [] for e in ENGINES}
        self._seen = {e: {} for e in ENGINES}
        self.self_ch = {}

    def emit(self, eng, fn, waits=(), inc=None):
        real = []
        seen = self._seen[eng]
        ch = self.self_ch.get(eng) if fn is not None else None
        if ch is not None:
            waits = list(waits) + [(ch.sem, ch.count)]
            inc = ch
        for w in waits:
            if w is None:
                continue
            sem, val = w
            if val <= 0 or seen.get(sem, -1) >= val:
                continue
            seen[sem] = val
            real.append((sem, val))
        incspec = None
        if inc is not None:
            inc.count += inc.step
            incspec = (inc.sem, inc.step)
        self.ops[eng].append((fn, tuple(real), incspec))
        return (inc.sem, inc.count) if inc is not None else None


def build_program():
    est = ExitStack()
    nc = bass.Bass("TRN2", target_bir_lowering=False)

    di = {}

    def inp(name, shape):
        di[name] = nc.dram_tensor(name, list(shape), F32, kind="ExternalInput").ap()
        return di[name]

    x_fm = inp("x_fm", (D, S))
    wqkvT = inp("wqkvT", (D, QKV))
    woT = inp("woT", (LAT, D))
    w1T = inp("w1T", (D, INTER))
    w3T = inp("w3T", (D, INTER))
    w2T = inp("w2T", (INTER, D))
    s1row = inp("s1row", (32, INTER))
    s3row = inp("s3row", (32, INTER))
    s2row = inp("s2row", (112, D))
    qc1L = inp("qc1L", (3, 8, 2, 128, 128))
    qc2L = inp("qc2L", (3, 8, 8, 128, 128))
    kc1L = inp("kc1L", (3, 2, 2, 128, 128))
    kc2L = inp("kc2L", (3, 2, 2, 128, 128))
    qbias = inp("qbias", (128, 16))
    kbias = inp("kbias", (128, 4))
    Mq = inp("Mq", (8, 128, 64))
    Mk = inp("Mk", (2, 128, 64))
    S5q = inp("S5q", (8, 64, 128))
    S5k = inp("S5k", (2, 64, 128))
    Oq = inp("Oq", (8, 128, 16))
    Ok = inp("Ok", (2, 128, 4))
    Bq = inp("Bq", (8, 16, 128))
    Bk = inp("Bk", (2, 4, 128))
    cosk = inp("cosk", (128, S))
    sink = inp("sink", (128, S))
    maskin = inp("maskin", (16, 128, OWN))
    ktemp = inp("ktemp", (128, 1))
    iodd = inp("iodd", (128, 1))
    ieven = inp("ieven", (128, 1))
    ident = inp("ident", (128, 128))

    y_out = nc.dram_tensor("y", [D, OWN], F32, kind="ExternalOutput").ap()

    wdq1 = nc.dram_tensor("wdq1", [D, INTER], BF16).ap()
    wdq3 = nc.dram_tensor("wdq3", [D, INTER], BF16).ap()
    wdq2 = nc.dram_tensor("wdq2", [INTER, D], BF16).ap()
    wqkvT_b = nc.dram_tensor("wqkvT_b", [D, QKV], BF16).ap()
    woT_b = nc.dram_tensor("woT_b", [LAT, D], BF16).ap()
    x1 = nc.dram_tensor("x1d", [D, S], F32).ap()
    qkp = nc.dram_tensor("qkp", [QKV, S + 8], F32).ap()
    kfin = nc.dram_tensor("kfin", [KVD, S], F32).ap()
    qfin = nc.dram_tensor("qfin", [LAT, OWN], F32).ap()
    x2 = nc.dram_tensor("x2d", [D, OWN], F32).ap()

    def dcol(j):
        return 4 + j if j < OWN else 8 + j

    sbufs = {}

    def sbuf(name, shape, dtype=F32):
        sbufs[name] = est.enter_context(nc.sbuf_tensor(name, list(shape), dtype))
        return sbufs[name]

    AR1 = sbuf("AR1", (128, 16384))
    AR2 = sbuf("AR2", (128, 16384))
    AR3 = sbuf("AR3", (128, 8192))
    AR4 = sbuf("AR4", (128, 5504))
    PCon = sbuf("PCon", (128, 2624))
    PRow = sbuf("PRow", (1, 4096))
    ps = est.enter_context(nc.psum_tensor("ps", [128, 4096], F32))

    def bank(b, p0=0, p1=128, c0=0, c1=512):
        return ps[p0:p1, b * 512 + c0:b * 512 + c1]

    onesA = PCon[:, 0:1]
    epsc = PCon[0:1, 1:2]
    negM = PCon[:, 2:3]
    ktc = PCon[:, 3:4]
    ioddc = PCon[:, 4:5]
    ievenc = PCon[:, 5:6]
    C_QB, C_KB, C_ID, C_MQ, C_MK, C_S5Q, C_S5K, C_OQ, C_OK = \
        6, 22, 26, 154, 666, 794, 1818, 2074, 2202
    identt = PCon[:, C_ID:C_ID + 128]

    def qb_col(j):
        return PCon[:, C_QB + j:C_QB + j + 1]

    def kb_col(j):
        return PCon[:, C_KB + j:C_KB + j + 1]

    def mq_t(c):
        return PCon[:, C_MQ + c * 64:C_MQ + (c + 1) * 64]

    def mk_t(c):
        return PCon[:, C_MK + c * 64:C_MK + (c + 1) * 64]

    def s5q_t(c):
        return PCon[0:64, C_S5Q + c * 128:C_S5Q + (c + 1) * 128]

    def s5k_t(c):
        return PCon[0:64, C_S5K + c * 128:C_S5K + (c + 1) * 128]

    def oq_t(c):
        return PCon[:, C_OQ + c * 16:C_OQ + (c + 1) * 16]

    def ok_t(c):
        return PCon[:, C_OK + c * 4:C_OK + (c + 1) * 4]

    BQ0 = 5504 - 1024 - 256 - 128
    BK0 = BQ0 + 1024
    ON0 = BK0 + 256

    def bq_t(c):
        return AR4[0:16, BQ0 + c * 128:BQ0 + (c + 1) * 128]

    def bk_t(c):
        return AR4[0:4, BK0 + c * 128:BK0 + (c + 1) * 128]

    ones1r = AR4[0:1, ON0:ON0 + 128]

    semX = est.enter_context(nc.semaphore("cX"))
    semP = est.enter_context(nc.semaphore("cP"))
    semV = est.enter_context(nc.semaphore("cV"))
    semA = est.enter_context(nc.semaphore("cA"))
    cX = Ch(semX, 16)
    cP, cV, cA = Ch(semP, 1), Ch(semV, 1), Ch(semA, 1)
    P = Prog()
    P.self_ch = {"vector": cV, "scalar": cA}
    emit = P.emit

    DCHS = [cX]

    def dch(name, n=1):
        out = []
        for i in range(n):
            c = Ch(est.enter_context(nc.semaphore(f"{name}{i}")), 16)
            DCHS.append(c)
            out.append(c)
        return out if n > 1 else out[0]

    dS2 = dch("dS", 2)
    dW4 = dch("dW", 4)
    dQ4 = dch("dQ", 4)
    lnX = dch("lnX", 4)
    fw4 = dch("fw", 4)
    fw24 = dch("fw2", 4)
    resX = dch("rX", 2)
    resY = dch("rY", 2)
    dCT = dch("dCT")
    qp4 = dch("qp", 4)
    kp2 = dch("kp", 2)
    dcs = dch("dcs", 2)
    dvt = dch("dvt", 2)
    dQR = dch("dQR")
    qi8 = dch("qi", 8)
    dKS = dch("dKS")
    qs2 = dch("qs", 2)
    mk2 = dch("mk", 2)
    st2 = dch("st", 2)

    def at(ch):
        return (ch.sem, ch.count)

    def barrier():
        waits = [at(c) for c in DCHS] + [at(cP), at(cV), at(cA)]
        for e in ENGINES:
            emit(e, None, waits=waits)

    def ld(dst, src):
        return emit("gpsimd", lambda g, d=dst, s=src: g.dma_start(out=d, in_=s), inc=cX)

    ld(ktc, ktemp)
    ld(ioddc, iodd)
    ld(ievenc, ieven)
    ld(PCon[:, C_QB:C_QB + 16], qbias)
    ld(PCon[:, C_KB:C_KB + 4], kbias)
    ld(identt, ident)
    for c in range(8):
        ld(mq_t(c), Mq[c])
        ld(s5q_t(c), S5q[c])
        ld(oq_t(c), Oq[c])
        ld(bq_t(c), Bq[c])
    for c in range(2):
        ld(mk_t(c), Mk[c])
        ld(s5k_t(c), S5k[c])
        ld(ok_t(c), Ok[c])
        ld(bk_t(c), Bk[c])
    lds = at(cX)
    emit("vector", lambda v: v.memset(onesA, 1.0), waits=[lds])
    emit("vector", lambda v: v.memset(ones1r, 1.0))
    emit("vector", lambda v: v.memset(epsc, 1e-5))
    emit("vector", lambda v: v.tensor_scalar_mul(negM, ktc, -1.0))
    emit("vector", lambda v: v.tensor_tensor(negM, negM, ktc, ALU.max))
    emit("vector", lambda v: v.tensor_scalar_mul(negM, negM, -0.125), inc=cV)
    barrier()

    def dequant(wsrc, srow_d, wdst, KB, N):
        srow = [PRow[0:1, 0:2048], PRow[0:1, 2048:4096]]
        wq_ring = [AR1[:, i * 512:(i + 1) * 512] for i in range(4)]
        dq_ring = [AR2[:, i * 256:(i + 1) * 256].bitcast(BF16) for i in range(4)]
        wq_free, dq_free, sr_free = {}, {}, {}
        nsp = N // 2048
        it = 0
        for kb in range(KB):
            for sp in range(nsp):
                si = kb * nsp + sp
                swl = [sr_free.get(si - 2)]
                ssnap = emit("gpsimd", lambda g, kb=kb, sp=sp, sl=si % 2: g.dma_start(
                    out=srow[sl], in_=srow_d[kb:kb + 1, sp * 2048:(sp + 1) * 2048]),
                    waits=swl, inc=dS2[si % 2])
                for j in range(4):
                    i = it
                    it += 1
                    wsnap = emit("sync", lambda s, kb=kb, sp=sp, j=j, sl=i % 4: s.dma_start(
                        out=wq_ring[sl],
                        in_=wsrc[kb * 128:(kb + 1) * 128, sp * 2048 + j * 512:sp * 2048 + (j + 1) * 512]),
                        waits=[wq_free.get(i - 4)], inc=dW4[i % 4])
                    psnap = emit("tensor", lambda t, j=j, sl=si % 2, bk=i % 2: t.matmul(
                        bank(bk), ones1r, srow[sl][0:1, j * 512:(j + 1) * 512],
                        start=True, stop=True),
                        waits=[ssnap, dq_free.get(i - 2)], inc=cP)
                    vsnap = emit("vector", lambda v, sl=i % 4, bk=i % 2: v.tensor_tensor(
                        dq_ring[sl], wq_ring[sl], bank(bk), ALU.mult),
                        waits=[wsnap, psnap, dq_free.get(i - 4)], inc=cV)
                    wq_free[i] = vsnap
                    stsnap = emit("gpsimd", lambda g, kb=kb, sp=sp, j=j, sl=i % 4: g.dma_start(
                        out=wdst[kb * 128:(kb + 1) * 128, sp * 2048 + j * 512:sp * 2048 + (j + 1) * 512],
                        in_=dq_ring[sl]),
                        waits=[vsnap], inc=dQ4[i % 4])
                    dq_free[i] = stsnap
                sr_free[si] = at(cP)

    dequant(w1T, s1row, wdq1, 32, INTER)
    barrier()
    dequant(w3T, s3row, wdq3, 32, INTER)
    barrier()
    dequant(w2T, s2row, wdq2, 112, D)
    barrier()

    def to_bf16(src, dst, KB, N):
        f_ring = [AR1[:, i * 512:(i + 1) * 512] for i in range(4)]
        b_ring = [AR2[:, i * 256:(i + 1) * 256].bitcast(BF16) for i in range(4)]
        ld_free, st_free = {}, {}
        it = 0
        for kb in range(KB):
            for j in range(N // 512):
                i = it
                it += 1
                ls = emit("sync", lambda s, kb=kb, j=j, sl=i % 4: s.dma_start(
                    out=f_ring[sl], in_=src[kb * 128:(kb + 1) * 128, j * 512:(j + 1) * 512]),
                    waits=[ld_free.get(i - 4)], inc=dW4[i % 4])
                cv = emit("vector", lambda v, sl=i % 4: v.tensor_copy(b_ring[sl], f_ring[sl]),
                          waits=[ls, st_free.get(i - 4)], inc=cV)
                ld_free[i] = cv
                st_free[i] = emit("gpsimd", lambda g, kb=kb, j=j, sl=i % 4: g.dma_start(
                    out=dst[kb * 128:(kb + 1) * 128, j * 512:(j + 1) * 512], in_=b_ring[sl]),
                    waits=[cv], inc=dQ4[i % 4])

    to_bf16(wqkvT, wqkvT_b, 32, QKV)
    barrier()
    to_bf16(woT, woT_b, 8, D)
    barrier()

    def ln_span(xsrc):
        xb = [AR4[:, i * 512:(i + 1) * 512] for i in range(4)]
        sqb = [AR4[:, (4 + i) * 512:(5 + i) * 512] for i in range(2)]
        xb_free, sq_free, sq_of = {}, {}, {}
        for c in range(NCH):
            xs = emit("gpsimd", lambda g, c=c: g.dma_start(out=xb[c % 4], in_=xsrc(c)),
                      waits=[xb_free.get(c - 4), sq_of.get(c - 4)], inc=lnX[c % 4])
            sq = emit("scalar", lambda s, c=c: s.activation(sqb[c % 2], xb[c % 4], AF.Square),
                      waits=[xs, sq_free.get(c - 2)], inc=cA)
            sq_of[c] = sq
            m1 = emit("tensor", lambda t, c=c: t.matmul(bank(0, 0, 1), onesA, xb[c % 4],
                                                        start=(c == 0), stop=(c == NCH - 1)),
                      waits=[xs], inc=cP)
            xb_free[c] = m1
            m2 = emit("tensor", lambda t, c=c: t.matmul(bank(1, 0, 1), onesA, sqb[c % 2],
                                                        start=(c == 0), stop=(c == NCH - 1)),
                      waits=[sq], inc=cP)
            sq_free[c] = m2
        sums = at(cP)
        meanr = PRow[0:1, 0:512]
        msqr = PRow[0:1, 512:1024]
        varr = PRow[0:1, 1024:1536]
        stdr = PRow[0:1, 1536:2048]
        rstdr = PRow[0:1, 2048:2560]
        emit("vector", lambda v: v.tensor_scalar_mul(meanr, bank(0, 0, 1), 1.0 / D), waits=[sums])
        emit("vector", lambda v: v.tensor_tensor(msqr, meanr, meanr, ALU.mult))
        vs = emit("vector", lambda v: v.scalar_tensor_tensor(varr, bank(1, 0, 1), 1.0 / D, msqr,
                                                             ALU.mult, ALU.subtract), inc=cV)
        ss = emit("scalar", lambda s: s.activation(stdr, varr, AF.Sqrt, bias=epsc), waits=[vs], inc=cA)
        rs = emit("vector", lambda v: v.reciprocal(rstdr, stdr), waits=[ss], inc=cV)
        bm = emit("tensor", lambda t: t.matmul(bank(2), ones1r, meanr, start=True, stop=True),
                  waits=[vs], inc=cP)
        br = emit("tensor", lambda t: t.matmul(bank(3), ones1r, rstdr, start=True, stop=True),
                  waits=[rs], inc=cP)
        meanb = AR4[:, 3072:3584]
        rstdb = AR4[:, 3584:4096]
        e1 = emit("vector", lambda v: v.tensor_copy(meanb, bank(2)), waits=[bm], inc=cV)
        e2 = emit("vector", lambda v: v.tensor_copy(rstdb, bank(3)), waits=[br], inc=cV)
        xb_free2 = {}
        for c in range(NCH):
            xs = emit("gpsimd", lambda g, c=c: g.dma_start(out=xb[c % 4], in_=xsrc(c)),
                      waits=[e1, e2, xb_free2.get(c - 4)], inc=lnX[c % 4])
            xn_c = xn_b(c)
            emit("vector", lambda v, c=c: v.tensor_tensor(xb[c % 4], xb[c % 4], meanb,
                                                          ALU.subtract),
                 waits=[xs])
            fs = emit("vector", lambda v, c=c, xn_c=xn_c: v.tensor_tensor(xn_c, xb[c % 4], rstdb,
                                                                          ALU.mult), inc=cV)
            xb_free2[c] = fs
        return at(cV)

    def xn_chunk(c):
        return AR1[:, c * 512:(c + 1) * 512]

    def xn_b(c):
        return AR1[:, c * 256:(c + 1) * 256].bitcast(BF16)

    def ffn_span(xsrc, ydst):
        xn_ready = ln_span(xsrc)
        yacc = AR2
        zm = emit("vector", lambda v: v.memset(yacc[:, 0:16384], 0.0), waits=[xn_ready], inc=cV)
        w_ring = [AR3[:, i * 512:(i + 1) * 512].bitcast(BF16) for i in range(4)]
        w2_ring = [AR3[:, 2048 + i * 512:2048 + (i + 1) * 512].bitcast(BF16) for i in range(4)]
        gs = [AR4[:, nb * 256:(nb + 1) * 256].bitcast(BF16) for nb in range(8)]
        sA = [AR4[:, 2048 + nb * 256:2048 + (nb + 1) * 256].bitcast(BF16) for nb in range(8)]
        wfree, w2free = {}, {}
        bank_free = [None] * 8
        for grp in range(14):
            for mat in range(2):
                wsrc_d = wdq1 if mat == 0 else wdq3
                for kb in range(32):
                    i = grp * 64 + mat * 32 + kb
                    wsn = emit("sync", lambda s, kb=kb, grp=grp, wsrc_d=wsrc_d, sl=i % 4: s.dma_start(
                        out=w_ring[sl],
                        in_=wsrc_d[kb * 128:(kb + 1) * 128, grp * 1024:(grp + 1) * 1024]),
                        waits=[xn_ready, wfree.get(i - 4)], inc=fw4[i % 4])
                    for nb in range(8):
                        wts = [wsn]
                        if kb == 0 and bank_free[nb] is not None:
                            wts.append(bank_free[nb])
                        sn = emit("tensor", lambda t, kb=kb, nb=nb, sl=i % 4: t.matmul(
                            bank(nb), w_ring[sl][:, nb * 128:(nb + 1) * 128], xn_b(kb),
                            start=(kb == 0), stop=(kb == 31)),
                            waits=wts, inc=(cP if (nb == 7 or kb == 31) else None))
                        if kb == 31:
                            if mat == 0:
                                bank_free[nb] = emit(
                                    "scalar", lambda s, nb=nb: s.activation(sA[nb], bank(nb), AF.Silu),
                                    waits=[sn], inc=cA)
                            else:
                                bank_free[nb] = emit(
                                    "vector", lambda v, nb=nb: v.tensor_tensor(gs[nb], sA[nb], bank(nb), ALU.mult),
                                    waits=[sn], inc=cV)
                    wfree[i] = at(cP)
            h_ready = at(cV)
            for ygrp in range(4):
                for kb8 in range(8):
                    i = grp * 32 + ygrp * 8 + kb8
                    wsn = emit("sync", lambda s, grp=grp, ygrp=ygrp, kb8=kb8, sl=i % 4: s.dma_start(
                        out=w2_ring[sl],
                        in_=wdq2[(grp * 8 + kb8) * 128:(grp * 8 + kb8 + 1) * 128,
                                 ygrp * 1024:(ygrp + 1) * 1024]),
                        waits=[h_ready, w2free.get(i - 4)], inc=fw24[i % 4])
                    for yc in range(8):
                        wts = [wsn]
                        if kb8 == 0 and bank_free[yc] is not None:
                            wts.append(bank_free[yc])
                        sn = emit("tensor", lambda t, kb8=kb8, yc=yc, sl=i % 4: t.matmul(
                            bank(yc), w2_ring[sl][:, yc * 128:(yc + 1) * 128], gs[kb8],
                            start=(kb8 == 0), stop=(kb8 == 7)),
                            waits=wts, inc=(cP if (yc == 7 or kb8 == 7) else None))
                        if kb8 == 7:
                            yo = (ygrp * 8 + yc) * 512
                            bank_free[yc] = emit("vector", lambda v, yo=yo, yc=yc: v.scalar_tensor_tensor(
                                yacc[:, yo:yo + 512], bank(yc), 1.0, yacc[:, yo:yo + 512],
                                ALU.mult, ALU.add),
                                waits=[sn, zm], inc=cV)
                    w2free[i] = at(cP)
        yacc_done = at(cV)
        xb = [AR4[:, i * 512:(i + 1) * 512] for i in range(2)]
        ob = [AR4[:, (2 + i) * 512:(3 + i) * 512] for i in range(2)]
        xfree, ofree = {}, {}
        for c in range(NCH):
            xs = emit("gpsimd", lambda g, c=c: g.dma_start(out=xb[c % 2], in_=xsrc(c)),
                      waits=[yacc_done, xfree.get(c - 2)], inc=resX[c % 2])
            vsn = emit("vector", lambda v, c=c: v.scalar_tensor_tensor(
                ob[c % 2], yacc[:, c * 512:(c + 1) * 512], 0.5, xb[c % 2], ALU.mult, ALU.add),
                waits=[xs, ofree.get(c - 2)], inc=cV)
            xfree[c] = vsn
            ofree[c] = emit("gpsimd", lambda g, c=c: g.dma_start(out=ydst(c), in_=ob[c % 2]),
                            waits=[vsn], inc=resY[c % 2])

    for sp in range(S // TSP):
        t0 = sp * TSP
        ffn_span(lambda c, t0=t0: x_fm[c * 128:(c + 1) * 128, t0:t0 + TSP],
                 lambda c, t0=t0: x1[c * 128:(c + 1) * 128, t0:t0 + TSP])
        barrier()

    for sp in range(S // TSP):
        t0 = sp * TSP
        xn_ready = ln_span(lambda c, t0=t0: x1[c * 128:(c + 1) * 128, t0:t0 + TSP])
        qkvbuf = [AR3[:, i * 384:(i + 1) * 384].bitcast(BF16) for i in range(4)]
        evb = [AR4[:, i * 512:(i + 1) * 512] for i in range(2)]
        wfree, evfree = {}, {}
        bank_free = [None] * 6
        for wave in range(2):
            for kb in range(32):
                i = wave * 32 + kb
                wsn = emit("sync", lambda s, kb=kb, wave=wave, sl=i % 4: s.dma_start(
                    out=qkvbuf[sl], in_=wqkvT_b[kb * 128:(kb + 1) * 128, wave * 768:(wave + 1) * 768]),
                    waits=[xn_ready, wfree.get(i - 4)], inc=fw4[i % 4])
                for nb in range(6):
                    wts = [wsn]
                    if kb == 0 and bank_free[nb] is not None:
                        wts.append(bank_free[nb])
                    sn = emit("tensor", lambda t, kb=kb, nb=nb, sl=i % 4: t.matmul(
                        bank(nb), qkvbuf[sl][:, nb * 128:(nb + 1) * 128], xn_b(kb),
                        start=(kb == 0), stop=(kb == 31)),
                        waits=wts, inc=(cP if (nb == 5 or kb == 31) else None))
                    if kb == 31:
                        j = wave * 6 + nb
                        ev = emit("vector", lambda v, nb=nb, j=j: v.tensor_copy(evb[j % 2], bank(nb)),
                                  waits=[sn, evfree.get(j - 2)], inc=cV)
                        bank_free[nb] = ev
                        evfree[j] = emit("gpsimd", lambda g, j=j, t0=t0: g.dma_start(
                            out=qkp[j * 128:(j + 1) * 128, dcol(t0):dcol(t0) + TSP], in_=evb[j % 2]),
                            waits=[ev], inc=resY[j % 2])
                wfree[i] = at(cP)
        barrier()

    hb = AR4[:, 0:8]
    for c in range(QKV // 128):
        s1 = emit("gpsimd", lambda g, c=c: g.dma_start(
            out=hb[:, 0:4], in_=qkp[c * 128:(c + 1) * 128, 8 + 2 * OWN - 4:8 + 2 * OWN]),
            inc=resX[0])
        s2 = emit("gpsimd", lambda g, c=c: g.dma_start(
            out=hb[:, 4:8], in_=qkp[c * 128:(c + 1) * 128, OWN:4 + OWN]), inc=resX[1])
        v1 = emit("vector", lambda v: v.tensor_scalar(hb[:, 0:4], hb[:, 0:4], ioddc, None, ALU.mult),
                  waits=[s1], inc=cV)
        v2 = emit("vector", lambda v: v.tensor_scalar(hb[:, 4:8], hb[:, 4:8], ievenc, None, ALU.mult),
                  waits=[s2], inc=cV)
        st1 = emit("gpsimd", lambda g, c=c: g.dma_start(out=qkp[c * 128:(c + 1) * 128, 0:4], in_=hb[:, 0:4]),
                   waits=[v1], inc=resY[0])
        emit("gpsimd", lambda g, c=c: g.dma_start(out=qkp[c * 128:(c + 1) * 128, 4 + OWN:8 + OWN],
                                                  in_=hb[:, 4:8]),
             waits=[v2], inc=resY[1])
        barrier()

    def load_tiles(dst, srcs):
        for idx, s_ in enumerate(srcs):
            emit("gpsimd", lambda g, idx=idx, s_=s_: g.dma_start(
                out=dst[:, idx * 128:(idx + 1) * 128], in_=s_), inc=dCT)
        return at(dCT)

    kc_srcs = [kc1L[dt, oc, src] for dt in range(3) for oc in range(2) for src in range(2)] + \
              [kc2L[dt, oc, src] for dt in range(3) for oc in range(2) for src in range(2)]
    kct = AR3[:, 0:24 * 128]
    kc_ready = load_tiles(kct, kc_srcs)

    def kc1_t(dt, oc, src):
        return kct[:, (dt * 4 + oc * 2 + src) * 128:(dt * 4 + oc * 2 + src + 1) * 128]

    def kc2_t(dt, oc, src):
        return kct[:, (12 + dt * 4 + oc * 2 + src) * 128:(13 + dt * 4 + oc * 2 + src) * 128]

    for slot in range(64):
        emit("vector", lambda v, slot=slot: v.memset(
            AR2[:, 2048 + slot * 65 + 64:2048 + slot * 65 + 65], 1.0))
    emit("vector", lambda v: v.memset(AR2[:, 0:1], 0.0), inc=cV)
    vone_ones = at(cV)

    for sp in range(S // TSP):
        t0 = sp * TSP
        dc0 = dcol(t0)
        rd0 = dc0 - 4
        kin = [AR1[:, i * 520:i * 520 + 516] for i in range(2)]
        k1o = [AR1[:, 1040 + i * 520:1040 + i * 520 + 514] for i in range(2)]
        k2o = [AR1[:, 2080 + i * 512:2080 + (i + 1) * 512] for i in range(2)]
        ksq = [AR1[:, 3104 + i * 512:3104 + (i + 1) * 512] for i in range(2)]
        qpb = [AR1[:, 4128 + i * 512:4128 + (i + 1) * 512] for i in range(4)]
        qm = AR1[0:64, 6176:6688]
        rnk = AR1[0:4, 6688:7200]
        kro = [AR1[:, 7200 + i * 512:7200 + (i + 1) * 512] for i in range(2)]
        vtile = [AR1[:, 8224 + i * 512:8224 + (i + 1) * 512] for i in range(2)]
        kins = []
        for ch in range(2):
            kins.append(emit("gpsimd", lambda g, ch=ch, rd0=rd0: g.dma_start(
                out=kin[ch], in_=qkp[LAT + ch * 128:LAT + (ch + 1) * 128, rd0:rd0 + 516]),
                waits=[kc_ready], inc=resX[ch]))
        msnap = None
        for c in range(8):
            s_ = emit("gpsimd", lambda g, c=c, dc0=dc0: g.dma_start(
                out=qpb[c % 4], in_=qkp[c * 128:(c + 1) * 128, dc0:dc0 + TSP]),
                waits=[msnap if c >= 4 else None], inc=qp4[c % 4])
            msnap = emit("tensor", lambda t, c=c: t.matmul(bank(0, 0, 64), mq_t(c), qpb[c % 4],
                                                           start=(c == 0), stop=(c == 7)),
                         waits=[s_], inc=cP)
        qme = emit("vector", lambda v: v.tensor_copy(qm, bank(0, 0, 64)), waits=[msnap], inc=cV)
        prev_evac = None
        for oc in range(2):
            for (go, gw, bk) in ((0, 512, 1), (512, 2, 2)):
                last = None
                for src in range(2):
                    for dt in range(3):
                        st_ = (src == 0 and dt == 0)
                        sp_ = (src == 1 and dt == 2)
                        wts = [kins[src]]
                        if st_ and prev_evac is not None:
                            wts.append(prev_evac)
                        last = emit("tensor",
                                    lambda t, oc=oc, src=src, dt=dt, go=go, gw=gw, bk=bk, st_=st_, sp_=sp_:
                                    t.matmul(bank(bk, 0, 128, 0, gw), kc1_t(dt, oc, src),
                                             kin[src][:, go + dt:go + dt + gw], start=st_, stop=sp_),
                                    waits=wts, inc=(cP if sp_ else None))
                prev_evac = emit("vector", lambda v, oc=oc, go=go, gw=gw, bk=bk: v.tensor_scalar(
                    k1o[oc][:, go:go + gw], bank(bk, 0, 128, 0, gw), kb_col(oc), None, ALU.add),
                    waits=[last], inc=cV)
        if sp == 0 or sp == 2:
            gatec = ioddc if sp == 0 else ievenc
            for oc in range(2):
                emit("vector", lambda v, oc=oc, gatec=gatec: v.tensor_scalar(
                    k1o[oc][:, 0:2], k1o[oc][:, 0:2], gatec, None, ALU.mult), inc=cV)
        k1_done = at(cV)
        for oc in range(2):
            last = None
            for src in range(2):
                for dt in range(3):
                    st_ = (src == 0 and dt == 0)
                    last = emit("tensor", lambda t, oc=oc, src=src, dt=dt, st_=st_: t.matmul(
                        bank(3 + oc), kc2_t(dt, oc, src), k1o[src][:, dt:dt + 512],
                        start=st_, stop=False),
                        waits=[k1_done])
            last = emit("tensor", lambda t, oc=oc: t.matmul(bank(3 + oc), s5k_t(oc), qm,
                                                            start=False, stop=True),
                        waits=[qme], inc=cP)
            emit("vector", lambda v, oc=oc: v.tensor_scalar(k2o[oc], bank(3 + oc), kb_col(2 + oc),
                                                            None, ALU.add),
                 waits=[last], inc=cV)
        k2_done = at(cV)
        sq1 = emit("scalar", lambda s: s.activation(ksq[0], k2o[0], AF.Square), waits=[k2_done], inc=cA)
        sq2 = emit("scalar", lambda s: s.activation(ksq[1], k2o[1], AF.Square), inc=cA)
        emit("tensor", lambda t: t.matmul(bank(5, 0, 4), ok_t(0), ksq[0], start=True, stop=False),
             waits=[sq1])
        o2 = emit("tensor", lambda t: t.matmul(bank(5, 0, 4), ok_t(1), ksq[1], start=False, stop=True),
                  waits=[sq2], inc=cP)
        s1_ = emit("scalar", lambda s: s.activation(rnk, bank(5, 0, 4), AF.Sqrt), waits=[o2], inc=cA)
        emit("vector", lambda v: v.tensor_scalar_max(rnk, rnk, 1e-12), waits=[s1_])
        emit("vector", lambda v: v.reciprocal(rnk, rnk))
        v3_ = emit("vector", lambda v: v.tensor_scalar(rnk, rnk, ktc[0:4, 0:1], 0.125,
                                                       ALU.mult, ALU.mult), inc=cV)
        prev_tt = None
        for ch in range(2):
            bb = emit("tensor", lambda t, ch=ch: t.matmul(bank(6), bk_t(ch), rnk, start=True, stop=True),
                      waits=[v3_] + ([prev_tt] if prev_tt else []), inc=cP)
            prev_tt = emit("vector", lambda v, ch=ch: v.tensor_tensor(k2o[ch], k2o[ch], bank(6), ALU.mult),
                           waits=[bb], inc=cV)
        kh_done = at(cV)
        cksb = AR2[:, 0:512]
        sksb = AR2[:, 512:1024]
        tmp1 = AR2[:, 1024:1536]
        tmp2 = AR2[:, 1536:2048]
        c1 = emit("gpsimd", lambda g, t0=t0: g.dma_start(out=cksb, in_=cosk[:, t0:t0 + TSP]),
                  inc=dcs[0])
        c2 = emit("gpsimd", lambda g, t0=t0: g.dma_start(out=sksb, in_=sink[:, t0:t0 + TSP]),
                  inc=dcs[1])
        emit("vector", lambda v: v.tensor_tensor(tmp1, k2o[0], cksb, ALU.mult), waits=[kh_done, c1, c2])
        emit("vector", lambda v: v.tensor_tensor(tmp2, k2o[1], sksb, ALU.mult))
        emit("vector", lambda v: v.tensor_tensor(kro[0], tmp1, tmp2, ALU.subtract))
        emit("vector", lambda v: v.tensor_tensor(tmp1, k2o[0], sksb, ALU.mult))
        emit("vector", lambda v: v.tensor_tensor(tmp2, k2o[1], cksb, ALU.mult))
        rp = emit("vector", lambda v: v.tensor_tensor(kro[1], tmp1, tmp2, ALU.add), inc=cV)
        for ch in range(2):
            emit("gpsimd", lambda g, ch=ch, t0=t0: g.dma_start(
                out=kfin[ch * 128:(ch + 1) * 128, t0:t0 + TSP], in_=kro[ch]), waits=[rp], inc=cX)
        vs1 = emit("gpsimd", lambda g, dc0=dc0: g.dma_start(
            out=vtile[0], in_=qkp[LAT + KVD:LAT + KVD + 128, dc0:dc0 + TSP]), inc=dvt[0])
        vs2 = emit("gpsimd", lambda g, rd0=rd0: g.dma_start(
            out=vtile[1], in_=qkp[LAT + KVD + 128:LAT + KVD + 256, rd0 + 3:rd0 + 3 + TSP]),
            inc=dvt[1])
        prev_cp = None
        for vch in range(2):
            for q4 in range(4):
                tci = sp * 4 + q4
                tr = emit("tensor", lambda t, vch=vch, q4=q4: t.transpose(
                    bank(7, 0, 128, 0, 128), vtile[vch][:, q4 * 128:(q4 + 1) * 128], identt),
                    waits=[vs1, vs2, vone_ones] + ([prev_cp] if prev_cp else []), inc=cP)
                for j in range(2):
                    slot = tci * 4 + vch * 2 + j
                    prev_cp = emit("vector", lambda v, j=j, slot=slot: v.tensor_copy(
                        AR2[:, 2048 + slot * 65:2048 + slot * 65 + 64],
                        bank(7, 0, 128, j * 64, (j + 1) * 64)),
                        waits=[tr], inc=cV)
        barrier()

    qc_srcs = [qc1L[dt, oc, src] for dt in range(3) for oc in range(8) for src in range(2)]
    qct = AR3[:, 0:48 * 128]
    qc_ready = load_tiles(qct, qc_srcs)

    def qc1_t(dt, oc, src):
        return qct[:, (dt * 16 + oc * 2 + src) * 128:(dt * 16 + oc * 2 + src + 1) * 128]

    q2ring = AR2[:, 6208:6208 + 24 * 128]

    for sp in range(OWN // TSP):
        t0 = sp * TSP
        dc0 = dcol(t0)
        rd0 = dc0 - 4
        qin = [AR1[:, i * 520:i * 520 + 516] for i in range(8)]
        q1o = [AR1[:, 4160 + i * 520:4160 + i * 520 + 514] for i in range(8)]
        q2o = [AR1[:, 8320 + i * 512:8320 + (i + 1) * 512] for i in range(8)]
        km = AR1[0:64, 12416:12928]
        rnq = AR1[0:16, 12928:13440]
        kpb = [AR1[:, 13440 + i * 512:13440 + (i + 1) * 512] for i in range(2)]
        qro = [AR1[:, 14464 + i * 512:14464 + (i + 1) * 512] for i in range(2)]
        qsq = [AR2[:, 9280 + i * 512:9280 + (i + 1) * 512] for i in range(2)]
        qins = []
        for c in range(8):
            qins.append(emit("gpsimd", lambda g, c=c, rd0=rd0: g.dma_start(
                out=qin[c], in_=qkp[c * 128:(c + 1) * 128, rd0:rd0 + 516]),
                waits=[qc_ready], inc=qi8[c]))
        m_ = None
        for c in range(2):
            s_ = emit("gpsimd", lambda g, c=c, dc0=dc0: g.dma_start(
                out=kpb[c], in_=qkp[LAT + c * 128:LAT + (c + 1) * 128, dc0:dc0 + TSP]),
                inc=kp2[c])
            m_ = emit("tensor", lambda t, c=c: t.matmul(bank(0, 0, 64), mk_t(c), kpb[c],
                                                        start=(c == 0), stop=(c == 1)),
                      waits=[s_], inc=cP)
        kme = emit("vector", lambda v: v.tensor_copy(km, bank(0, 0, 64)), waits=[m_], inc=cV)
        prev_evac = None
        for oc in range(8):
            for (go, gw, bk) in ((0, 512, 1), (512, 2, 2)):
                last = None
                for src in range(2):
                    ic = (oc % 4) + src * 4
                    for dt in range(3):
                        st_ = (src == 0 and dt == 0)
                        sp_ = (src == 1 and dt == 2)
                        wts = [qins[ic]]
                        if st_ and prev_evac is not None:
                            wts.append(prev_evac)
                        last = emit("tensor",
                                    lambda t, oc=oc, ic=ic, src=src, dt=dt, go=go, gw=gw, bk=bk, st_=st_, sp_=sp_:
                                    t.matmul(bank(bk, 0, 128, 0, gw), qc1_t(dt, oc, src),
                                             qin[ic][:, go + dt:go + dt + gw], start=st_, stop=sp_),
                                    waits=wts, inc=(cP if sp_ else None))
                prev_evac = emit("vector", lambda v, oc=oc, go=go, gw=gw, bk=bk: v.tensor_scalar(
                    q1o[oc][:, go:go + gw], bank(bk, 0, 128, 0, gw), qb_col(oc), None, ALU.add),
                    waits=[last], inc=cV)
        if sp == 0:
            for oc in range(8):
                emit("vector", lambda v, oc=oc: v.tensor_scalar(
                    q1o[oc][:, 0:2], q1o[oc][:, 0:2], ioddc, None, ALU.mult), inc=cV)
        q1_done = at(cV)
        prev_evac2 = None
        for oc in range(8):
            tl = [(dt, ic) for ic in range(8) for dt in range(3)]
            for idx, (dt, ic) in enumerate(tl):
                emit("gpsimd", lambda g, idx=idx, dt=dt, ic=ic, oc=oc: g.dma_start(
                    out=q2ring[:, idx * 128:(idx + 1) * 128], in_=qc2L[dt, oc, ic]),
                    waits=[prev_evac2] if idx == 0 else [], inc=dQR)
            lsn_all = at(dQR)
            last = None
            for idx, (dt, ic) in enumerate(tl):
                st_ = (idx == 0)
                wts = [lsn_all, q1_done]
                if st_ and prev_evac2 is not None:
                    wts.append(prev_evac2)
                last = emit("tensor", lambda t, idx=idx, dt=dt, ic=ic, st_=st_: t.matmul(
                    bank(3), q2ring[:, idx * 128:(idx + 1) * 128], q1o[ic][:, dt:dt + 512],
                    start=st_, stop=False),
                    waits=wts)
            last = emit("tensor", lambda t, oc=oc: t.matmul(bank(3), s5q_t(oc), km,
                                                            start=False, stop=True),
                        waits=[kme], inc=cP)
            prev_evac2 = emit("vector", lambda v, oc=oc: v.tensor_scalar(
                q2o[oc], bank(3), qb_col(8 + oc), None, ALU.add),
                waits=[last], inc=cV)
        q2_done = at(cV)
        osn = {}
        m_ = None
        for c in range(8):
            sq = emit("scalar", lambda s, c=c: s.activation(qsq[c % 2], q2o[c], AF.Square),
                      waits=[q2_done, osn.get(c - 2)], inc=cA)
            m_ = emit("tensor", lambda t, c=c: t.matmul(bank(5, 0, 16), oq_t(c), qsq[c % 2],
                                                        start=(c == 0), stop=(c == 7)),
                      waits=[sq], inc=cP)
            osn[c] = m_
        s1_ = emit("scalar", lambda s: s.activation(rnq, bank(5, 0, 16), AF.Sqrt), waits=[m_], inc=cA)
        emit("vector", lambda v: v.tensor_scalar_max(rnq, rnq, 1e-12), waits=[s1_])
        v3_ = emit("vector", lambda v: v.reciprocal(rnq, rnq), inc=cV)
        prev_tt = None
        for ch in range(8):
            bb = emit("tensor", lambda t, ch=ch: t.matmul(bank(6), bq_t(ch), rnq, start=True, stop=True),
                      waits=[v3_] + ([prev_tt] if prev_tt else []), inc=cP)
            prev_tt = emit("vector", lambda v, ch=ch: v.tensor_tensor(q2o[ch], q2o[ch], bank(6), ALU.mult),
                           waits=[bb], inc=cV)
        qh_done = at(cV)
        cksb = AR2[:, 9280 + 1024:9280 + 1536]
        sksb = AR2[:, 9280 + 1536:9280 + 2048]
        tmp1 = AR2[:, 9280 + 2048:9280 + 2560]
        tmp2 = AR2[:, 9280 + 2560:9280 + 3072]
        c1 = emit("gpsimd", lambda g, t0=t0: g.dma_start(out=cksb, in_=cosk[:, t0:t0 + TSP]),
                  inc=dcs[0])
        c2 = emit("gpsimd", lambda g, t0=t0: g.dma_start(out=sksb, in_=sink[:, t0:t0 + TSP]),
                  inc=dcs[1])
        prev_st = None
        for pc in range(4):
            e, o = q2o[pc], q2o[pc + 4]
            emit("vector", lambda v, e=e: v.tensor_tensor(tmp1, e, cksb, ALU.mult),
                 waits=[qh_done, c1, c2] + ([prev_st] if prev_st else []))
            emit("vector", lambda v, o=o: v.tensor_tensor(tmp2, o, sksb, ALU.mult))
            emit("vector", lambda v: v.tensor_tensor(qro[0], tmp1, tmp2, ALU.subtract))
            emit("vector", lambda v, e=e: v.tensor_tensor(tmp1, e, sksb, ALU.mult))
            emit("vector", lambda v, o=o: v.tensor_tensor(tmp2, o, cksb, ALU.mult))
            rp = emit("vector", lambda v: v.tensor_tensor(qro[1], tmp1, tmp2, ALU.add), inc=cV)
            emit("gpsimd", lambda g, pc=pc, t0=t0: g.dma_start(
                out=qfin[pc * 128:(pc + 1) * 128, t0:t0 + TSP], in_=qro[0]), waits=[rp], inc=cX)
            prev_st = emit("gpsimd", lambda g, pc=pc, t0=t0: g.dma_start(
                out=qfin[(4 + pc) * 128:(5 + pc) * 128, t0:t0 + TSP], in_=qro[1]),
                waits=[rp], inc=cX)
        barrier()

    for kvh in range(4):
        emit("gpsimd", lambda g, kvh=kvh: g.dma_start(
            out=AR1[0:32, kvh * 2048:(kvh + 1) * 2048], in_=kfin[32 * kvh:32 * kvh + 32, :]),
            inc=dKS)
        emit("gpsimd", lambda g, kvh=kvh: g.dma_start(
            out=AR1[32:64, kvh * 2048:(kvh + 1) * 2048],
            in_=kfin[128 + 32 * kvh:128 + 32 * kvh + 32, :]), inc=dKS)
    kst_ready = at(dKS)
    aout = AR2[0:64, 6208:6208 + 16 * 256].bitcast(BF16)
    for sp in range(OWN // TSP):
        t0 = sp * TSP
        qst = [AR1[:, 8192 + i * 512:8192 + (i + 1) * 512] for i in range(2)]
        expb = [AR1[:, 9216 + i * 512:9216 + (i + 1) * 512] for i in range(2)]
        maskb = [AR1[:, 10240 + i * 512:10240 + (i + 1) * 512] for i in range(2)]
        avs = AR1[0:64, 11264:11776]
        x1b = [AR1[:, 11776 + i * 512:11776 + (i + 1) * 512] for i in range(2)]
        x2b = [AR1[:, 12800 + i * 512:12800 + (i + 1) * 512] for i in range(2)]
        qfree = {}
        mask_done = {}
        exp_done = {}
        avlast = None
        zblast = None
        for h in range(NH):
            kv = h // 4
            gi0 = h * 16
            emit("gpsimd", lambda g, h=h, t0=t0: g.dma_start(
                out=qst[h % 2][0:32, :], in_=qfin[32 * h:32 * h + 32, t0:t0 + TSP]),
                waits=[kst_ready, qfree.get(h - 2)], inc=qs2[h % 2])
            emit("gpsimd", lambda g, h=h, t0=t0: g.dma_start(
                out=qst[h % 2][32:64, :], in_=qfin[512 + 32 * h:512 + 32 * h + 32, t0:t0 + TSP]),
                inc=qs2[h % 2])
            qrdy = at(qs2[h % 2])
            for ci in range(16):
                gi = gi0 + ci
                sb_ = ci % 2
                msn = emit("gpsimd", lambda g, ci=ci, sb_=sb_, t0=t0: g.dma_start(
                    out=maskb[sb_], in_=maskin[ci][:, t0:t0 + TSP]),
                    waits=[mask_done.get(gi - 2)], inc=mk2[sb_])
                sc = emit("tensor", lambda t, ci=ci, kv=kv, sb_=sb_, h=h: t.matmul(
                    bank(sb_), AR1[0:64, kv * 2048 + ci * 128:kv * 2048 + (ci + 1) * 128],
                    qst[h % 2][0:64, :], start=True, stop=True, tile_position=(0, 0)),
                    waits=[qrdy, exp_done.get(gi - 2)], inc=cP)
                ex = emit("scalar", lambda s, sb_=sb_: s.activation(expb[sb_], bank(sb_), AF.Exp,
                                                                    bias=negM),
                          waits=[sc], inc=cA)
                exp_done[gi] = ex
                mm_ = emit("vector", lambda v, sb_=sb_: v.tensor_tensor(expb[sb_], expb[sb_],
                                                                        maskb[sb_], ALU.mult),
                           waits=[ex, msn], inc=cV)
                mask_done[gi] = mm_
                slot = ci * 4 + kv
                avw = [mm_]
                if ci == 0 and avlast is not None:
                    avw.append(avlast)
                av = emit("tensor", lambda t, slot=slot, sb_=sb_, ci=ci: t.matmul(
                    bank(3, 0, 65), AR2[:, 2048 + slot * 65:2048 + (slot + 1) * 65], expb[sb_],
                    start=(ci == 0), stop=(ci == 15)),
                    waits=avw, inc=(cP if ci == 15 else None))
                if ci == 15:
                    av_done = av
            qfree[h] = at(cP)
            zr = PRow[0:1, 3072:3584]
            rz = emit("vector", lambda v: v.reciprocal(zr, bank(3, 64, 65)), waits=[av_done], inc=cV)
            zb = emit("tensor", lambda t: t.matmul(bank(4, 0, 64), ones1r[0:1, 0:64], zr,
                                                   start=True, stop=True),
                      waits=[rz] + ([zblast] if zblast else []), inc=cP)
            ae = emit("scalar", lambda s: s.copy(avs, bank(3, 0, 64)), waits=[rz], inc=cA)
            avlast = ae
            zblast = emit("vector", lambda v, h=h: v.tensor_tensor(
                aout[:, h * 512:(h + 1) * 512], avs, bank(4, 0, 64), ALU.mult),
                waits=[zb, ae], inc=cV)
        aout_done = at(cV)
        wob = [AR3[:, i * 384:(i + 1) * 384].bitcast(BF16) for i in range(4)]
        wofree = {}
        bank_free2 = [None] * 6
        xfree, x2free = {}, {}
        for wave in range(6):
            ncb = 6 if wave < 5 else 2
            for h in range(NH):
                i = wave * 16 + h
                wsn = emit("sync", lambda s, wave=wave, h=h, sl=i % 4, ncb=ncb: s.dma_start(
                    out=wob[sl][0:64, 0:ncb * 128],
                    in_=woT_b[64 * h:64 * h + 64, wave * 768:wave * 768 + ncb * 128]),
                    waits=[aout_done, wofree.get(i - 4)], inc=fw4[i % 4])
                for nb in range(ncb):
                    wts = [wsn]
                    if h == 0 and bank_free2[nb] is not None:
                        wts.append(bank_free2[nb])
                    sn = emit("tensor", lambda t, h=h, nb=nb, sl=i % 4: t.matmul(
                        bank(nb), wob[sl][0:64, nb * 128:(nb + 1) * 128],
                        aout[:, h * 512:(h + 1) * 512],
                        start=(h == 0), stop=(h == NH - 1), tile_position=(0, 0)),
                        waits=wts, inc=(cP if (nb == ncb - 1 or h == NH - 1) else None))
                    if h == NH - 1:
                        c = wave * 6 + nb
                        xs = emit("gpsimd", lambda g, c=c, t0=t0: g.dma_start(
                            out=x1b[c % 2], in_=x1[c * 128:(c + 1) * 128, t0:t0 + TSP]),
                            waits=[xfree.get(c - 2)], inc=resX[c % 2])
                        vv = emit("vector", lambda v, c=c, nb=nb: v.scalar_tensor_tensor(
                            x2b[c % 2], bank(nb), 1.0, x1b[c % 2], ALU.mult, ALU.add),
                            waits=[sn, xs, x2free.get(c - 2)], inc=cV)
                        bank_free2[nb] = vv
                        xfree[c] = vv
                        x2free[c] = emit("gpsimd", lambda g, c=c, t0=t0: g.dma_start(
                            out=x2[c * 128:(c + 1) * 128, t0:t0 + TSP], in_=x2b[c % 2]),
                            waits=[vv], inc=st2[c % 2])
                wofree[i] = at(cP)
        barrier()

    for sp in range(OWN // TSP):
        t0 = sp * TSP
        ffn_span(lambda c, t0=t0: x2[c * 128:(c + 1) * 128, t0:t0 + TSP],
                 lambda c, t0=t0: y_out[c * 128:(c + 1) * 128, t0:t0 + TSP])
        barrier()

    emit("gpsimd", None, waits=[at(c) for c in DCHS])

    with nc.Block() as block:
        for ename in ENGINES:
            def make(ename):
                def run(eng):
                    for fn, waits, incspec in P.ops[ename]:
                        for sem, val in waits:
                            eng.wait_ge(sem, val)
                        if fn is None:
                            continue
                        instr = fn(eng)
                        if incspec is not None:
                            instr.then_inc(*incspec)
                return run
            getattr(block, ename)(make(ename))
    est.close()
    return nc



def _deint_perm(nheads):
    n = nheads * HD
    perm = np.zeros(n, dtype=np.int64)
    half = n // 2
    for h in range(nheads):
        for i in range(32):
            perm[h * 32 + i] = h * HD + 2 * i
            perm[half + h * 32 + i] = h * HD + 2 * i + 1
    return perm


def prep_inputs(inputs):
    f = lambda k: np.asarray(inputs[k], dtype=np.float32)
    x = f("L0_attn_x")
    wqkv = f("L0_attn_w_qkv")
    wo = f("L0_attn_w_o")
    qc1w, qc1b = f("L0_attn_qc1_w"), f("L0_attn_qc1_b")
    qc2w, qc2b = f("L0_attn_qc2_w"), f("L0_attn_qc2_b")
    kc1w, kc1b = f("L0_attn_kc1_w"), f("L0_attn_kc1_b")
    kc2w, kc2b = f("L0_attn_kc2_w"), f("L0_attn_kc2_b")
    ktemp = f("L0_attn_key_temp")
    w1, w1s = f("L0_ffn_w1"), f("L0_ffn_w1_scale")
    w3, w3s = f("L0_ffn_w3"), f("L0_ffn_w3_scale")
    w2, w2s = f("L0_ffn_w2"), f("L0_ffn_w2_scale")

    qperm = _deint_perm(NH)
    kperm = _deint_perm(NKV)
    rows = np.concatenate([qperm, LAT + kperm, np.arange(LAT + KVD, QKV)])
    wqkvT = np.ascontiguousarray(wqkv[rows, :].T)
    woT = np.ascontiguousarray(wo.T)
    w1T = np.ascontiguousarray(w1.T)
    w3T = np.ascontiguousarray(w3.T)
    w2T = np.ascontiguousarray(w2.T)
    s1row = np.ascontiguousarray(np.repeat(w1s, 128, axis=0).T)
    s3row = np.ascontiguousarray(np.repeat(w3s, 128, axis=0).T)
    s2row = np.ascontiguousarray(np.repeat(w2s, 128, axis=0).T)

    def conv1_tiles(w, b, nheads, perm):
        C = nheads * HD
        nch = C // 128
        nch_half = (C // 2) // 128
        dense = np.zeros((C, C, 3), dtype=np.float32)
        for o in range(C):
            g = o // HD
            dense[o, g * HD:(g + 1) * HD, :] = w[o]
        dd = dense[perm][:, perm]
        tiles = np.zeros((3, nch, 2, 128, 128), dtype=np.float32)
        for dt in range(3):
            for oc in range(nch):
                for src in range(2):
                    ic = (oc % nch_half) + src * nch_half
                    tiles[dt, oc, src] = dd[oc * 128:(oc + 1) * 128,
                                            ic * 128:(ic + 1) * 128, dt].T
        return np.ascontiguousarray(tiles), np.ascontiguousarray(b[perm])

    def conv2_tiles(w, b, C, perm):
        nch = C // 128
        dd = w[perm][:, perm]
        tiles = np.zeros((3, nch, nch, 128, 128), dtype=np.float32)
        for dt in range(3):
            for oc in range(nch):
                for ic in range(nch):
                    tiles[dt, oc, ic] = dd[oc * 128:(oc + 1) * 128,
                                           ic * 128:(ic + 1) * 128, dt].T
        return np.ascontiguousarray(tiles), np.ascontiguousarray(b[perm])

    qc1L, qb1 = conv1_tiles(qc1w, qc1b, NH, qperm)
    qc2L, qb2 = conv2_tiles(qc2w, qc2b, LAT, qperm)
    kc1L, kb1 = conv1_tiles(kc1w, kc1b, NKV, kperm)
    kc2L, kb2 = conv2_tiles(kc2w, kc2b, KVD, kperm)
    qbias = np.zeros((128, 16), dtype=np.float32)
    for c in range(8):
        qbias[:, c] = qb1[c * 128:(c + 1) * 128]
        qbias[:, 8 + c] = qb2[c * 128:(c + 1) * 128]
    kbias = np.zeros((128, 4), dtype=np.float32)
    for c in range(2):
        kbias[:, c] = kb1[c * 128:(c + 1) * 128]
        kbias[:, 2 + c] = kb2[c * 128:(c + 1) * 128]

    Mq = np.zeros((8, 128, 64), dtype=np.float32)
    Mk = np.zeros((2, 128, 64), dtype=np.float32)
    S5q = np.zeros((8, 64, 128), dtype=np.float32)
    S5k = np.zeros((2, 64, 128), dtype=np.float32)
    Oq = np.zeros((8, 128, 16), dtype=np.float32)
    Ok = np.zeros((2, 128, 4), dtype=np.float32)
    Bq = np.zeros((8, 16, 128), dtype=np.float32)
    Bk = np.zeros((2, 4, 128), dtype=np.float32)
    for c in range(8):
        for r in range(128):
            i = r % 32
            d0 = 0 if c < 4 else 32
            Mq[c, r, d0 + i] = 1.0 / NH
            S5q[c, d0 + i, r] = 0.5
            h = (c % 4) * 4 + r // 32
            Oq[c, r, h] = 1.0
            Bq[c, h, r] = 1.0
    for c in range(2):
        for r in range(128):
            i = r % 32
            d0 = 0 if c < 1 else 32
            Mk[c, r, d0 + i] = 1.0 / NKV
            S5k[c, d0 + i, r] = 0.5
            h = r // 32
            Ok[c, r, h] = 1.0
            Bk[c, h, r] = 1.0

    ident = np.eye(128, dtype=np.float32)
    shared = dict(wqkvT=wqkvT, woT=woT, w1T=w1T, w3T=w3T, w2T=w2T,
                  s1row=s1row, s3row=s3row, s2row=s2row,
                  qc1L=qc1L, qc2L=qc2L, kc1L=kc1L, kc2L=kc2L,
                  qbias=qbias, kbias=kbias, Mq=Mq, Mk=Mk, S5q=S5q, S5k=S5k,
                  Oq=Oq, Ok=Ok, Bq=Bq, Bk=Bk, ident=ident,
                  ktemp=np.broadcast_to(ktemp.reshape(1, 1), (128, 1)).copy())

    inv = 1.0 / (10000.0 ** (np.arange(0, HD, 2, dtype=np.float32) / HD))
    ang = np.arange(S, dtype=np.float32)[:, None] * inv[None, :]
    cosg = np.cos(ang).T.astype(np.float32)
    sing = np.sin(ang).T.astype(np.float32)

    in_maps = []
    for core in range(8):
        b, h = core // 2, core % 2
        off = h * OWN
        order = np.concatenate([np.arange(off, off + OWN), np.arange(0, off),
                                np.arange(off + OWN, S)])
        gt = order.astype(np.float32)
        m = dict(shared)
        m["x_fm"] = np.ascontiguousarray(x[b].T[:, order])
        m["cosk"] = np.ascontiguousarray(np.tile(cosg[:, order], (4, 1)))
        m["sink"] = np.ascontiguousarray(np.tile(sing[:, order], (4, 1)))
        maskin = np.zeros((16, 128, OWN), dtype=np.float32)
        for ci in range(16):
            maskin[ci] = (gt[ci * 128:(ci + 1) * 128][:, None] <= gt[None, :OWN]).astype(np.float32)
        m["maskin"] = maskin
        m["iodd"] = np.full((128, 1), float(h == 1), dtype=np.float32)
        m["ieven"] = np.full((128, 1), float(h == 0), dtype=np.float32)
        in_maps.append(m)
    return in_maps


def kernel(**inputs):
    in_maps = prep_inputs(inputs)
    nc = build_program()
    res = run_bass_kernel_spmd(nc, in_maps, core_ids=list(range(8)))
    out = np.zeros((4, S, D), dtype=np.float32)
    for core in range(8):
        b, h = core // 2, core % 2
        y = res.results[core]["y"]
        out[b, h * OWN:(h + 1) * OWN, :] = np.asarray(y).T
    return out


# version 04 (final)
# speedup vs baseline: 2.4019x; 1.0357x over previous
import numpy as np
from contextlib import ExitStack

import concourse.bass as bass
import concourse.mybir as mybir
from concourse.bass_utils import run_bass_kernel_spmd

F32 = mybir.dt.float32
BF16 = mybir.dt.bfloat16
AF = mybir.ActivationFunctionType
ALU = mybir.AluOpType

D = 4096
INTER = 14336
S = 2048
OWN = 1024
NH, NKV, HD = 16, 4, 64
LAT = NH * HD
KVD = NKV * HD
QKV = LAT + 2 * KVD
TSP = 512
NCH = D // 128

ENGINES = ("sync", "tensor", "vector", "scalar", "gpsimd")


class Ch:
    def __init__(self, sem, step):
        self.sem = sem
        self.step = step
        self.count = 0


class Prog:
    def __init__(self):
        self.ops = {e: [] for e in ENGINES}
        self._seen = {e: {} for e in ENGINES}
        self.self_ch = {}

    def emit(self, eng, fn, waits=(), inc=None):
        real = []
        seen = self._seen[eng]
        ch = self.self_ch.get(eng) if fn is not None else None
        if ch is not None:
            waits = list(waits) + [(ch.sem, ch.count)]
            inc = ch
        for w in waits:
            if w is None:
                continue
            sem, val = w
            if val <= 0 or seen.get(sem, -1) >= val:
                continue
            seen[sem] = val
            real.append((sem, val))
        incspec = None
        if inc is not None:
            inc.count += inc.step
            incspec = (inc.sem, inc.step)
        self.ops[eng].append((fn, tuple(real), incspec))
        return (inc.sem, inc.count) if inc is not None else None


def build_program():
    est = ExitStack()
    nc = bass.Bass("TRN2", target_bir_lowering=False)

    di = {}

    def inp(name, shape):
        di[name] = nc.dram_tensor(name, list(shape), F32, kind="ExternalInput").ap()
        return di[name]

    x_fm = inp("x_fm", (D, S))
    wqkvT = inp("wqkvT", (D, QKV))
    woT = inp("woT", (LAT, D))
    w1T = inp("w1T", (D, INTER))
    w3T = inp("w3T", (D, INTER))
    w2T = inp("w2T", (INTER, D))
    s1row = inp("s1row", (32, INTER))
    s3row = inp("s3row", (32, INTER))
    s2row = inp("s2row", (112, D))
    qc1L = inp("qc1L", (3, 8, 2, 128, 128))
    qc2L = inp("qc2L", (3, 8, 8, 128, 128))
    kc1L = inp("kc1L", (3, 2, 2, 128, 128))
    kc2L = inp("kc2L", (3, 2, 2, 128, 128))
    qbias = inp("qbias", (128, 16))
    kbias = inp("kbias", (128, 4))
    Mq = inp("Mq", (8, 128, 64))
    Mk = inp("Mk", (2, 128, 64))
    S5q = inp("S5q", (8, 64, 128))
    S5k = inp("S5k", (2, 64, 128))
    Oq = inp("Oq", (8, 128, 16))
    Ok = inp("Ok", (2, 128, 4))
    Bq = inp("Bq", (8, 16, 128))
    Bk = inp("Bk", (2, 4, 128))
    cosk = inp("cosk", (128, S))
    sink = inp("sink", (128, S))
    maskin = inp("maskin", (16, 128, OWN))
    ktemp = inp("ktemp", (128, 1))
    iodd = inp("iodd", (128, 1))
    ieven = inp("ieven", (128, 1))
    ident = inp("ident", (128, 128))

    y_out = nc.dram_tensor("y", [D, OWN], F32, kind="ExternalOutput").ap()

    wdq1 = nc.dram_tensor("wdq1", [D, INTER], BF16).ap()
    wdq3 = nc.dram_tensor("wdq3", [D, INTER], BF16).ap()
    wdq2 = nc.dram_tensor("wdq2", [INTER, D], BF16).ap()
    wqkvT_b = nc.dram_tensor("wqkvT_b", [D, QKV], BF16).ap()
    woT_b = nc.dram_tensor("woT_b", [LAT, D], BF16).ap()
    x1 = nc.dram_tensor("x1d", [D, S], F32).ap()
    qkp = nc.dram_tensor("qkp", [QKV, S + 8], F32).ap()
    kfin = nc.dram_tensor("kfin", [KVD, S], BF16).ap()
    qfin = nc.dram_tensor("qfin", [LAT, OWN], BF16).ap()
    x2 = nc.dram_tensor("x2d", [D, OWN], F32).ap()

    def dcol(j):
        return 4 + j if j < OWN else 8 + j

    sbufs = {}

    def sbuf(name, shape, dtype=F32):
        sbufs[name] = est.enter_context(nc.sbuf_tensor(name, list(shape), dtype))
        return sbufs[name]

    AR1 = sbuf("AR1", (128, 16384))
    AR2 = sbuf("AR2", (128, 16384))
    AR3 = sbuf("AR3", (128, 8192))
    AR4 = sbuf("AR4", (128, 5504))
    PCon = sbuf("PCon", (128, 2624))
    PRow = sbuf("PRow", (1, 4096))
    ps = est.enter_context(nc.psum_tensor("ps", [128, 4096], F32))

    def bank(b, p0=0, p1=128, c0=0, c1=512):
        return ps[p0:p1, b * 512 + c0:b * 512 + c1]

    onesA = PCon[:, 0:1]
    epsc = PCon[0:1, 1:2]
    negM = PCon[:, 2:3]
    ktc = PCon[:, 3:4]
    ioddc = PCon[:, 4:5]
    ievenc = PCon[:, 5:6]
    C_QB, C_KB, C_ID, C_MQ, C_MK, C_S5Q, C_S5K, C_OQ, C_OK = \
        6, 22, 26, 154, 666, 794, 1818, 2074, 2202
    identt = PCon[:, C_ID:C_ID + 128]

    def qb_col(j):
        return PCon[:, C_QB + j:C_QB + j + 1]

    def kb_col(j):
        return PCon[:, C_KB + j:C_KB + j + 1]

    def mq_t(c):
        return PCon[:, C_MQ + c * 64:C_MQ + (c + 1) * 64]

    def mk_t(c):
        return PCon[:, C_MK + c * 64:C_MK + (c + 1) * 64]

    def s5q_t(c):
        return PCon[0:64, C_S5Q + c * 128:C_S5Q + (c + 1) * 128]

    def s5k_t(c):
        return PCon[0:64, C_S5K + c * 128:C_S5K + (c + 1) * 128]

    def oq_t(c):
        return PCon[:, C_OQ + c * 16:C_OQ + (c + 1) * 16]

    def ok_t(c):
        return PCon[:, C_OK + c * 4:C_OK + (c + 1) * 4]

    BQ0 = 5504 - 1024 - 256 - 128
    BK0 = BQ0 + 1024
    ON0 = BK0 + 256

    def bq_t(c):
        return AR4[0:16, BQ0 + c * 128:BQ0 + (c + 1) * 128]

    def bk_t(c):
        return AR4[0:4, BK0 + c * 128:BK0 + (c + 1) * 128]

    ones1r = AR4[0:1, ON0:ON0 + 128]

    semX = est.enter_context(nc.semaphore("cX"))
    semP = est.enter_context(nc.semaphore("cP"))
    semV = est.enter_context(nc.semaphore("cV"))
    semA = est.enter_context(nc.semaphore("cA"))
    cX = Ch(semX, 16)
    cP, cV, cA = Ch(semP, 1), Ch(semV, 1), Ch(semA, 1)
    P = Prog()
    P.self_ch = {"vector": cV, "scalar": cA}
    emit = P.emit

    DCHS = [cX]

    def dch(name, n=1):
        out = []
        for i in range(n):
            c = Ch(est.enter_context(nc.semaphore(f"{name}{i}")), 16)
            DCHS.append(c)
            out.append(c)
        return out if n > 1 else out[0]

    dS2 = dch("dS", 2)
    dW4 = dch("dW", 4)
    dQ4 = dch("dQ", 4)
    lnX = dch("lnX", 4)
    fw4 = dch("fw", 4)
    fw24 = dch("fw2", 4)
    resX = dch("rX", 2)
    resY = dch("rY", 2)
    dCT = dch("dCT")
    qp4 = dch("qp", 4)
    kp2 = dch("kp", 2)
    dcs = dch("dcs", 2)
    dvt = dch("dvt", 2)
    dQR = dch("dQR")
    qi8 = dch("qi", 8)
    dKS = dch("dKS")
    qs2 = dch("qs", 2)
    mk2 = dch("mk", 2)
    st2 = dch("st", 2)

    def at(ch):
        return (ch.sem, ch.count)

    def barrier():
        waits = [at(c) for c in DCHS] + [at(cP), at(cV), at(cA)]
        for e in ENGINES:
            emit(e, None, waits=waits)

    def ld(dst, src):
        return emit("gpsimd", lambda g, d=dst, s=src: g.dma_start(out=d, in_=s), inc=cX)

    ld(ktc, ktemp)
    ld(ioddc, iodd)
    ld(ievenc, ieven)
    ld(PCon[:, C_QB:C_QB + 16], qbias)
    ld(PCon[:, C_KB:C_KB + 4], kbias)
    ld(identt, ident)
    for c in range(8):
        ld(mq_t(c), Mq[c])
        ld(s5q_t(c), S5q[c])
        ld(oq_t(c), Oq[c])
        ld(bq_t(c), Bq[c])
    for c in range(2):
        ld(mk_t(c), Mk[c])
        ld(s5k_t(c), S5k[c])
        ld(ok_t(c), Ok[c])
        ld(bk_t(c), Bk[c])
    lds = at(cX)
    emit("vector", lambda v: v.memset(onesA, 1.0), waits=[lds])
    emit("vector", lambda v: v.memset(ones1r, 1.0))
    emit("vector", lambda v: v.memset(epsc, 1e-5))
    emit("vector", lambda v: v.tensor_scalar_mul(negM, ktc, -1.0))
    emit("vector", lambda v: v.tensor_tensor(negM, negM, ktc, ALU.max))
    emit("vector", lambda v: v.tensor_scalar_mul(negM, negM, -0.125), inc=cV)
    barrier()

    def dequant(wsrc, srow_d, wdst, KB, N):
        srow = [PRow[0:1, 0:2048], PRow[0:1, 2048:4096]]
        wq_ring = [AR1[:, i * 512:(i + 1) * 512] for i in range(4)]
        dq_ring = [AR2[:, i * 256:(i + 1) * 256].bitcast(BF16) for i in range(4)]
        wq_free, dq_free, sr_free = {}, {}, {}
        nsp = N // 2048
        it = 0
        for kb in range(KB):
            for sp in range(nsp):
                si = kb * nsp + sp
                swl = [sr_free.get(si - 2)]
                ssnap = emit("gpsimd", lambda g, kb=kb, sp=sp, sl=si % 2: g.dma_start(
                    out=srow[sl], in_=srow_d[kb:kb + 1, sp * 2048:(sp + 1) * 2048]),
                    waits=swl, inc=dS2[si % 2])
                for j in range(4):
                    i = it
                    it += 1
                    wsnap = emit("sync", lambda s, kb=kb, sp=sp, j=j, sl=i % 4: s.dma_start(
                        out=wq_ring[sl],
                        in_=wsrc[kb * 128:(kb + 1) * 128, sp * 2048 + j * 512:sp * 2048 + (j + 1) * 512]),
                        waits=[wq_free.get(i - 4)], inc=dW4[i % 4])
                    psnap = emit("tensor", lambda t, j=j, sl=si % 2, bk=i % 2: t.matmul(
                        bank(bk), ones1r, srow[sl][0:1, j * 512:(j + 1) * 512],
                        start=True, stop=True),
                        waits=[ssnap, dq_free.get(i - 2)], inc=cP)
                    vsnap = emit("vector", lambda v, sl=i % 4, bk=i % 2: v.tensor_tensor(
                        dq_ring[sl], wq_ring[sl], bank(bk), ALU.mult),
                        waits=[wsnap, psnap, dq_free.get(i - 4)], inc=cV)
                    wq_free[i] = vsnap
                    stsnap = emit("gpsimd", lambda g, kb=kb, sp=sp, j=j, sl=i % 4: g.dma_start(
                        out=wdst[kb * 128:(kb + 1) * 128, sp * 2048 + j * 512:sp * 2048 + (j + 1) * 512],
                        in_=dq_ring[sl]),
                        waits=[vsnap], inc=dQ4[i % 4])
                    dq_free[i] = stsnap
                sr_free[si] = at(cP)

    dequant(w1T, s1row, wdq1, 32, INTER)
    barrier()
    dequant(w3T, s3row, wdq3, 32, INTER)
    barrier()
    dequant(w2T, s2row, wdq2, 112, D)
    barrier()

    def to_bf16(src, dst, KB, N):
        f_ring = [AR1[:, i * 512:(i + 1) * 512] for i in range(4)]
        b_ring = [AR2[:, i * 256:(i + 1) * 256].bitcast(BF16) for i in range(4)]
        ld_free, st_free = {}, {}
        it = 0
        for kb in range(KB):
            for j in range(N // 512):
                i = it
                it += 1
                ls = emit("sync", lambda s, kb=kb, j=j, sl=i % 4: s.dma_start(
                    out=f_ring[sl], in_=src[kb * 128:(kb + 1) * 128, j * 512:(j + 1) * 512]),
                    waits=[ld_free.get(i - 4)], inc=dW4[i % 4])
                cv = emit("vector", lambda v, sl=i % 4: v.tensor_copy(b_ring[sl], f_ring[sl]),
                          waits=[ls, st_free.get(i - 4)], inc=cV)
                ld_free[i] = cv
                st_free[i] = emit("gpsimd", lambda g, kb=kb, j=j, sl=i % 4: g.dma_start(
                    out=dst[kb * 128:(kb + 1) * 128, j * 512:(j + 1) * 512], in_=b_ring[sl]),
                    waits=[cv], inc=dQ4[i % 4])

    to_bf16(wqkvT, wqkvT_b, 32, QKV)
    barrier()
    to_bf16(woT, woT_b, 8, D)
    barrier()

    def ln_span(xsrc):
        xb = [AR4[:, i * 512:(i + 1) * 512] for i in range(4)]
        sqb = [AR4[:, (4 + i) * 512:(5 + i) * 512] for i in range(2)]
        xb_free, sq_free, sq_of = {}, {}, {}
        for c in range(NCH):
            xs = emit("gpsimd", lambda g, c=c: g.dma_start(out=xb[c % 4], in_=xsrc(c)),
                      waits=[xb_free.get(c - 4), sq_of.get(c - 4)], inc=lnX[c % 4])
            sq = emit("scalar", lambda s, c=c: s.activation(sqb[c % 2], xb[c % 4], AF.Square),
                      waits=[xs, sq_free.get(c - 2)], inc=cA)
            sq_of[c] = sq
            m1 = emit("tensor", lambda t, c=c: t.matmul(bank(0, 0, 1), onesA, xb[c % 4],
                                                        start=(c == 0), stop=(c == NCH - 1)),
                      waits=[xs], inc=cP)
            xb_free[c] = m1
            m2 = emit("tensor", lambda t, c=c: t.matmul(bank(1, 0, 1), onesA, sqb[c % 2],
                                                        start=(c == 0), stop=(c == NCH - 1)),
                      waits=[sq], inc=cP)
            sq_free[c] = m2
        sums = at(cP)
        meanr = PRow[0:1, 0:512]
        msqr = PRow[0:1, 512:1024]
        varr = PRow[0:1, 1024:1536]
        stdr = PRow[0:1, 1536:2048]
        rstdr = PRow[0:1, 2048:2560]
        emit("vector", lambda v: v.tensor_scalar_mul(meanr, bank(0, 0, 1), 1.0 / D), waits=[sums])
        emit("vector", lambda v: v.tensor_tensor(msqr, meanr, meanr, ALU.mult))
        vs = emit("vector", lambda v: v.scalar_tensor_tensor(varr, bank(1, 0, 1), 1.0 / D, msqr,
                                                             ALU.mult, ALU.subtract), inc=cV)
        ss = emit("scalar", lambda s: s.activation(stdr, varr, AF.Sqrt, bias=epsc), waits=[vs], inc=cA)
        rs = emit("vector", lambda v: v.reciprocal(rstdr, stdr), waits=[ss], inc=cV)
        bm = emit("tensor", lambda t: t.matmul(bank(2), ones1r, meanr, start=True, stop=True),
                  waits=[vs], inc=cP)
        br = emit("tensor", lambda t: t.matmul(bank(3), ones1r, rstdr, start=True, stop=True),
                  waits=[rs], inc=cP)
        meanb = AR4[:, 3072:3584]
        rstdb = AR4[:, 3584:4096]
        e1 = emit("vector", lambda v: v.tensor_copy(meanb, bank(2)), waits=[bm], inc=cV)
        e2 = emit("vector", lambda v: v.tensor_copy(rstdb, bank(3)), waits=[br], inc=cV)
        xb_free2 = {}
        for c in range(NCH):
            xs = emit("gpsimd", lambda g, c=c: g.dma_start(out=xb[c % 4], in_=xsrc(c)),
                      waits=[e1, e2, xb_free2.get(c - 4)], inc=lnX[c % 4])
            xn_c = xn_b(c)
            emit("vector", lambda v, c=c: v.tensor_tensor(xb[c % 4], xb[c % 4], meanb,
                                                          ALU.subtract),
                 waits=[xs])
            fs = emit("vector", lambda v, c=c, xn_c=xn_c: v.tensor_tensor(xn_c, xb[c % 4], rstdb,
                                                                          ALU.mult), inc=cV)
            xb_free2[c] = fs
        return at(cV)

    def xn_chunk(c):
        return AR1[:, c * 512:(c + 1) * 512]

    def xn_b(c):
        return AR1[:, c * 256:(c + 1) * 256].bitcast(BF16)

    def ffn_span(xsrc, ydst):
        xn_ready = ln_span(xsrc)
        yacc = AR2
        zm = emit("vector", lambda v: v.memset(yacc[:, 0:16384], 0.0), waits=[xn_ready], inc=cV)
        w_ring = [AR3[:, i * 512:(i + 1) * 512].bitcast(BF16) for i in range(4)]
        w2_ring = [AR3[:, 2048 + i * 512:2048 + (i + 1) * 512].bitcast(BF16) for i in range(4)]
        gs = [AR4[:, nb * 256:(nb + 1) * 256].bitcast(BF16) for nb in range(8)]
        sA = [AR4[:, 2048 + nb * 256:2048 + (nb + 1) * 256].bitcast(BF16) for nb in range(8)]
        wfree, w2free = {}, {}
        bank_free = [None] * 8
        for grp in range(14):
            for mat in range(2):
                wsrc_d = wdq1 if mat == 0 else wdq3
                for kb in range(32):
                    i = grp * 64 + mat * 32 + kb
                    wsn = emit("sync", lambda s, kb=kb, grp=grp, wsrc_d=wsrc_d, sl=i % 4: s.dma_start(
                        out=w_ring[sl],
                        in_=wsrc_d[kb * 128:(kb + 1) * 128, grp * 1024:(grp + 1) * 1024]),
                        waits=[xn_ready, wfree.get(i - 4)], inc=fw4[i % 4])
                    for nb in range(8):
                        wts = [wsn]
                        if kb == 0 and bank_free[nb] is not None:
                            wts.append(bank_free[nb])
                        sn = emit("tensor", lambda t, kb=kb, nb=nb, sl=i % 4: t.matmul(
                            bank(nb), w_ring[sl][:, nb * 128:(nb + 1) * 128], xn_b(kb),
                            start=(kb == 0), stop=(kb == 31)),
                            waits=wts, inc=(cP if (nb == 7 or kb == 31) else None))
                        if kb == 31:
                            if mat == 0:
                                bank_free[nb] = emit(
                                    "scalar", lambda s, nb=nb: s.activation(sA[nb], bank(nb), AF.Silu),
                                    waits=[sn], inc=cA)
                            else:
                                bank_free[nb] = emit(
                                    "vector", lambda v, nb=nb: v.tensor_tensor(gs[nb], sA[nb], bank(nb), ALU.mult),
                                    waits=[sn], inc=cV)
                    wfree[i] = at(cP)
            h_ready = at(cV)
            for ygrp in range(4):
                for kb8 in range(8):
                    i = grp * 32 + ygrp * 8 + kb8
                    wsn = emit("sync", lambda s, grp=grp, ygrp=ygrp, kb8=kb8, sl=i % 4: s.dma_start(
                        out=w2_ring[sl],
                        in_=wdq2[(grp * 8 + kb8) * 128:(grp * 8 + kb8 + 1) * 128,
                                 ygrp * 1024:(ygrp + 1) * 1024]),
                        waits=[h_ready, w2free.get(i - 4)], inc=fw24[i % 4])
                    for yc in range(8):
                        wts = [wsn]
                        if kb8 == 0 and bank_free[yc] is not None:
                            wts.append(bank_free[yc])
                        sn = emit("tensor", lambda t, kb8=kb8, yc=yc, sl=i % 4: t.matmul(
                            bank(yc), w2_ring[sl][:, yc * 128:(yc + 1) * 128], gs[kb8],
                            start=(kb8 == 0), stop=(kb8 == 7)),
                            waits=wts, inc=(cP if (yc == 7 or kb8 == 7) else None))
                        if kb8 == 7:
                            yo = (ygrp * 8 + yc) * 512
                            bank_free[yc] = emit("vector", lambda v, yo=yo, yc=yc: v.scalar_tensor_tensor(
                                yacc[:, yo:yo + 512], bank(yc), 1.0, yacc[:, yo:yo + 512],
                                ALU.mult, ALU.add),
                                waits=[sn, zm], inc=cV)
                    w2free[i] = at(cP)
        yacc_done = at(cV)
        xb = [AR4[:, i * 512:(i + 1) * 512] for i in range(2)]
        ob = [AR4[:, (2 + i) * 512:(3 + i) * 512] for i in range(2)]
        xfree, ofree = {}, {}
        for c in range(NCH):
            xs = emit("gpsimd", lambda g, c=c: g.dma_start(out=xb[c % 2], in_=xsrc(c)),
                      waits=[yacc_done, xfree.get(c - 2)], inc=resX[c % 2])
            vsn = emit("vector", lambda v, c=c: v.scalar_tensor_tensor(
                ob[c % 2], yacc[:, c * 512:(c + 1) * 512], 0.5, xb[c % 2], ALU.mult, ALU.add),
                waits=[xs, ofree.get(c - 2)], inc=cV)
            xfree[c] = vsn
            ofree[c] = emit("gpsimd", lambda g, c=c: g.dma_start(out=ydst(c), in_=ob[c % 2]),
                            waits=[vsn], inc=resY[c % 2])

    for sp in range(S // TSP):
        t0 = sp * TSP
        ffn_span(lambda c, t0=t0: x_fm[c * 128:(c + 1) * 128, t0:t0 + TSP],
                 lambda c, t0=t0: x1[c * 128:(c + 1) * 128, t0:t0 + TSP])
        barrier()

    for sp in range(S // TSP):
        t0 = sp * TSP
        xn_ready = ln_span(lambda c, t0=t0: x1[c * 128:(c + 1) * 128, t0:t0 + TSP])
        qkvbuf = [AR3[:, i * 384:(i + 1) * 384].bitcast(BF16) for i in range(4)]
        evb = [AR4[:, i * 512:(i + 1) * 512] for i in range(2)]
        wfree, evfree = {}, {}
        bank_free = [None] * 6
        for wave in range(2):
            for kb in range(32):
                i = wave * 32 + kb
                wsn = emit("sync", lambda s, kb=kb, wave=wave, sl=i % 4: s.dma_start(
                    out=qkvbuf[sl], in_=wqkvT_b[kb * 128:(kb + 1) * 128, wave * 768:(wave + 1) * 768]),
                    waits=[xn_ready, wfree.get(i - 4)], inc=fw4[i % 4])
                for nb in range(6):
                    wts = [wsn]
                    if kb == 0 and bank_free[nb] is not None:
                        wts.append(bank_free[nb])
                    sn = emit("tensor", lambda t, kb=kb, nb=nb, sl=i % 4: t.matmul(
                        bank(nb), qkvbuf[sl][:, nb * 128:(nb + 1) * 128], xn_b(kb),
                        start=(kb == 0), stop=(kb == 31)),
                        waits=wts, inc=(cP if (nb == 5 or kb == 31) else None))
                    if kb == 31:
                        j = wave * 6 + nb
                        ev = emit("vector", lambda v, nb=nb, j=j: v.tensor_copy(evb[j % 2], bank(nb)),
                                  waits=[sn, evfree.get(j - 2)], inc=cV)
                        bank_free[nb] = ev
                        evfree[j] = emit("gpsimd", lambda g, j=j, t0=t0: g.dma_start(
                            out=qkp[j * 128:(j + 1) * 128, dcol(t0):dcol(t0) + TSP], in_=evb[j % 2]),
                            waits=[ev], inc=resY[j % 2])
                wfree[i] = at(cP)
        barrier()

    hb = AR4[:, 0:8]
    for c in range(QKV // 128):
        s1 = emit("gpsimd", lambda g, c=c: g.dma_start(
            out=hb[:, 0:4], in_=qkp[c * 128:(c + 1) * 128, 8 + 2 * OWN - 4:8 + 2 * OWN]),
            inc=resX[0])
        s2 = emit("gpsimd", lambda g, c=c: g.dma_start(
            out=hb[:, 4:8], in_=qkp[c * 128:(c + 1) * 128, OWN:4 + OWN]), inc=resX[1])
        v1 = emit("vector", lambda v: v.tensor_scalar(hb[:, 0:4], hb[:, 0:4], ioddc, None, ALU.mult),
                  waits=[s1], inc=cV)
        v2 = emit("vector", lambda v: v.tensor_scalar(hb[:, 4:8], hb[:, 4:8], ievenc, None, ALU.mult),
                  waits=[s2], inc=cV)
        st1 = emit("gpsimd", lambda g, c=c: g.dma_start(out=qkp[c * 128:(c + 1) * 128, 0:4], in_=hb[:, 0:4]),
                   waits=[v1], inc=resY[0])
        emit("gpsimd", lambda g, c=c: g.dma_start(out=qkp[c * 128:(c + 1) * 128, 4 + OWN:8 + OWN],
                                                  in_=hb[:, 4:8]),
             waits=[v2], inc=resY[1])
        barrier()

    def load_tiles(dst, srcs):
        for idx, s_ in enumerate(srcs):
            emit("gpsimd", lambda g, idx=idx, s_=s_: g.dma_start(
                out=dst[:, idx * 128:(idx + 1) * 128], in_=s_), inc=dCT)
        return at(dCT)

    kc_srcs = [kc1L[dt, oc, src] for dt in range(3) for oc in range(2) for src in range(2)] + \
              [kc2L[dt, oc, src] for dt in range(3) for oc in range(2) for src in range(2)]
    kct = AR3[:, 0:24 * 128]
    kc_ready = load_tiles(kct, kc_srcs)

    def kc1_t(dt, oc, src):
        return kct[:, (dt * 4 + oc * 2 + src) * 128:(dt * 4 + oc * 2 + src + 1) * 128]

    def kc2_t(dt, oc, src):
        return kct[:, (12 + dt * 4 + oc * 2 + src) * 128:(13 + dt * 4 + oc * 2 + src) * 128]

    def vone_b(slot, a, b):
        return AR2[:, 2048 + slot * 33:2048 + slot * 33 + 33].bitcast(BF16)[:, a:b]

    for slot in range(64):
        emit("vector", lambda v, slot=slot: v.memset(vone_b(slot, 64, 65), 1.0))
    emit("vector", lambda v: v.memset(AR2[:, 0:1], 0.0), inc=cV)
    vone_ones = at(cV)

    for sp in range(S // TSP):
        t0 = sp * TSP
        dc0 = dcol(t0)
        rd0 = dc0 - 4
        kin = [AR1[:, i * 520:i * 520 + 516] for i in range(2)]
        k1o = [AR1[:, 1040 + i * 520:1040 + i * 520 + 514] for i in range(2)]
        k2o = [AR1[:, 2080 + i * 512:2080 + (i + 1) * 512] for i in range(2)]
        ksq = [AR1[:, 3104 + i * 512:3104 + (i + 1) * 512] for i in range(2)]
        qpb = [AR1[:, 4128 + i * 512:4128 + (i + 1) * 512] for i in range(4)]
        qm = AR1[0:64, 6176:6688]
        rnk = AR1[0:4, 6688:7200]
        kro = [AR1[:, 7200 + i * 256:7200 + (i + 1) * 256].bitcast(BF16) for i in range(2)]
        vtile = [AR1[:, 8224 + i * 512:8224 + (i + 1) * 512] for i in range(2)]
        kins = []
        for ch in range(2):
            kins.append(emit("gpsimd", lambda g, ch=ch, rd0=rd0: g.dma_start(
                out=kin[ch], in_=qkp[LAT + ch * 128:LAT + (ch + 1) * 128, rd0:rd0 + 516]),
                waits=[kc_ready], inc=resX[ch]))
        msnap = None
        for c in range(8):
            s_ = emit("gpsimd", lambda g, c=c, dc0=dc0: g.dma_start(
                out=qpb[c % 4], in_=qkp[c * 128:(c + 1) * 128, dc0:dc0 + TSP]),
                waits=[msnap if c >= 4 else None], inc=qp4[c % 4])
            msnap = emit("tensor", lambda t, c=c: t.matmul(bank(0, 0, 64), mq_t(c), qpb[c % 4],
                                                           start=(c == 0), stop=(c == 7)),
                         waits=[s_], inc=cP)
        qme = emit("vector", lambda v: v.tensor_copy(qm, bank(0, 0, 64)), waits=[msnap], inc=cV)
        prev_evac = None
        for oc in range(2):
            for (go, gw, bk) in ((0, 512, 1), (512, 2, 2)):
                last = None
                for src in range(2):
                    for dt in range(3):
                        st_ = (src == 0 and dt == 0)
                        sp_ = (src == 1 and dt == 2)
                        wts = [kins[src]]
                        if st_ and prev_evac is not None:
                            wts.append(prev_evac)
                        last = emit("tensor",
                                    lambda t, oc=oc, src=src, dt=dt, go=go, gw=gw, bk=bk, st_=st_, sp_=sp_:
                                    t.matmul(bank(bk, 0, 128, 0, gw), kc1_t(dt, oc, src),
                                             kin[src][:, go + dt:go + dt + gw], start=st_, stop=sp_),
                                    waits=wts, inc=(cP if sp_ else None))
                prev_evac = emit("vector", lambda v, oc=oc, go=go, gw=gw, bk=bk: v.tensor_scalar(
                    k1o[oc][:, go:go + gw], bank(bk, 0, 128, 0, gw), kb_col(oc), None, ALU.add),
                    waits=[last], inc=cV)
        if sp == 0 or sp == 2:
            gatec = ioddc if sp == 0 else ievenc
            for oc in range(2):
                emit("vector", lambda v, oc=oc, gatec=gatec: v.tensor_scalar(
                    k1o[oc][:, 0:2], k1o[oc][:, 0:2], gatec, None, ALU.mult), inc=cV)
        k1_done = at(cV)
        for oc in range(2):
            last = None
            for src in range(2):
                for dt in range(3):
                    st_ = (src == 0 and dt == 0)
                    last = emit("tensor", lambda t, oc=oc, src=src, dt=dt, st_=st_: t.matmul(
                        bank(3 + oc), kc2_t(dt, oc, src), k1o[src][:, dt:dt + 512],
                        start=st_, stop=False),
                        waits=[k1_done])
            last = emit("tensor", lambda t, oc=oc: t.matmul(bank(3 + oc), s5k_t(oc), qm,
                                                            start=False, stop=True),
                        waits=[qme], inc=cP)
            emit("vector", lambda v, oc=oc: v.tensor_scalar(k2o[oc], bank(3 + oc), kb_col(2 + oc),
                                                            None, ALU.add),
                 waits=[last], inc=cV)
        k2_done = at(cV)
        sq1 = emit("scalar", lambda s: s.activation(ksq[0], k2o[0], AF.Square), waits=[k2_done], inc=cA)
        sq2 = emit("scalar", lambda s: s.activation(ksq[1], k2o[1], AF.Square), inc=cA)
        emit("tensor", lambda t: t.matmul(bank(5, 0, 4), ok_t(0), ksq[0], start=True, stop=False),
             waits=[sq1])
        o2 = emit("tensor", lambda t: t.matmul(bank(5, 0, 4), ok_t(1), ksq[1], start=False, stop=True),
                  waits=[sq2], inc=cP)
        s1_ = emit("scalar", lambda s: s.activation(rnk, bank(5, 0, 4), AF.Sqrt), waits=[o2], inc=cA)
        emit("vector", lambda v: v.tensor_scalar_max(rnk, rnk, 1e-12), waits=[s1_])
        emit("vector", lambda v: v.reciprocal(rnk, rnk))
        v3_ = emit("vector", lambda v: v.tensor_scalar(rnk, rnk, ktc[0:4, 0:1], 0.125,
                                                       ALU.mult, ALU.mult), inc=cV)
        prev_tt = None
        for ch in range(2):
            bb = emit("tensor", lambda t, ch=ch: t.matmul(bank(6), bk_t(ch), rnk, start=True, stop=True),
                      waits=[v3_] + ([prev_tt] if prev_tt else []), inc=cP)
            prev_tt = emit("vector", lambda v, ch=ch: v.tensor_tensor(k2o[ch], k2o[ch], bank(6), ALU.mult),
                           waits=[bb], inc=cV)
        kh_done = at(cV)
        cksb = AR2[:, 0:512]
        sksb = AR2[:, 512:1024]
        tmp1 = AR2[:, 1024:1536]
        tmp2 = AR2[:, 1536:2048]
        c1 = emit("gpsimd", lambda g, t0=t0: g.dma_start(out=cksb, in_=cosk[:, t0:t0 + TSP]),
                  inc=dcs[0])
        c2 = emit("gpsimd", lambda g, t0=t0: g.dma_start(out=sksb, in_=sink[:, t0:t0 + TSP]),
                  inc=dcs[1])
        emit("vector", lambda v: v.tensor_tensor(tmp1, k2o[0], cksb, ALU.mult), waits=[kh_done, c1, c2])
        emit("vector", lambda v: v.tensor_tensor(tmp2, k2o[1], sksb, ALU.mult))
        emit("vector", lambda v: v.tensor_tensor(kro[0], tmp1, tmp2, ALU.subtract))
        emit("vector", lambda v: v.tensor_tensor(tmp1, k2o[0], sksb, ALU.mult))
        emit("vector", lambda v: v.tensor_tensor(tmp2, k2o[1], cksb, ALU.mult))
        rp = emit("vector", lambda v: v.tensor_tensor(kro[1], tmp1, tmp2, ALU.add), inc=cV)
        for ch in range(2):
            emit("gpsimd", lambda g, ch=ch, t0=t0: g.dma_start(
                out=kfin[ch * 128:(ch + 1) * 128, t0:t0 + TSP], in_=kro[ch]), waits=[rp], inc=cX)
        vs1 = emit("gpsimd", lambda g, dc0=dc0: g.dma_start(
            out=vtile[0], in_=qkp[LAT + KVD:LAT + KVD + 128, dc0:dc0 + TSP]), inc=dvt[0])
        vs2 = emit("gpsimd", lambda g, rd0=rd0: g.dma_start(
            out=vtile[1], in_=qkp[LAT + KVD + 128:LAT + KVD + 256, rd0 + 3:rd0 + 3 + TSP]),
            inc=dvt[1])
        prev_cp = None
        for vch in range(2):
            for q4 in range(4):
                tci = sp * 4 + q4
                tr = emit("tensor", lambda t, vch=vch, q4=q4: t.transpose(
                    bank(7, 0, 128, 0, 128), vtile[vch][:, q4 * 128:(q4 + 1) * 128], identt),
                    waits=[vs1, vs2, vone_ones] + ([prev_cp] if prev_cp else []), inc=cP)
                for j in range(2):
                    slot = tci * 4 + vch * 2 + j
                    prev_cp = emit("vector", lambda v, j=j, slot=slot: v.tensor_copy(
                        vone_b(slot, 0, 64),
                        bank(7, 0, 128, j * 64, (j + 1) * 64)),
                        waits=[tr], inc=cV)
        barrier()

    qc_srcs = [qc1L[dt, oc, src] for dt in range(3) for oc in range(8) for src in range(2)]
    qct = AR3[:, 0:48 * 128]
    qc_ready = load_tiles(qct, qc_srcs)

    def qc1_t(dt, oc, src):
        return qct[:, (dt * 16 + oc * 2 + src) * 128:(dt * 16 + oc * 2 + src + 1) * 128]

    q2ring = AR2[:, 6208:6208 + 24 * 128]

    for sp in range(OWN // TSP):
        t0 = sp * TSP
        dc0 = dcol(t0)
        rd0 = dc0 - 4
        qin = [AR1[:, i * 520:i * 520 + 516] for i in range(8)]
        q1o = [AR1[:, 4160 + i * 520:4160 + i * 520 + 514] for i in range(8)]
        q2o = [AR1[:, 8320 + i * 512:8320 + (i + 1) * 512] for i in range(8)]
        km = AR1[0:64, 12416:12928]
        rnq = AR1[0:16, 12928:13440]
        kpb = [AR1[:, 13440 + i * 512:13440 + (i + 1) * 512] for i in range(2)]
        qro = [AR1[:, 14464 + i * 256:14464 + (i + 1) * 256].bitcast(BF16) for i in range(2)]
        qsq = [AR2[:, 9280 + i * 512:9280 + (i + 1) * 512] for i in range(2)]
        qins = []
        for c in range(8):
            qins.append(emit("gpsimd", lambda g, c=c, rd0=rd0: g.dma_start(
                out=qin[c], in_=qkp[c * 128:(c + 1) * 128, rd0:rd0 + 516]),
                waits=[qc_ready], inc=qi8[c]))
        m_ = None
        for c in range(2):
            s_ = emit("gpsimd", lambda g, c=c, dc0=dc0: g.dma_start(
                out=kpb[c], in_=qkp[LAT + c * 128:LAT + (c + 1) * 128, dc0:dc0 + TSP]),
                inc=kp2[c])
            m_ = emit("tensor", lambda t, c=c: t.matmul(bank(0, 0, 64), mk_t(c), kpb[c],
                                                        start=(c == 0), stop=(c == 1)),
                      waits=[s_], inc=cP)
        kme = emit("vector", lambda v: v.tensor_copy(km, bank(0, 0, 64)), waits=[m_], inc=cV)
        prev_evac = None
        for oc in range(8):
            for (go, gw, bk) in ((0, 512, 1), (512, 2, 2)):
                last = None
                for src in range(2):
                    ic = (oc % 4) + src * 4
                    for dt in range(3):
                        st_ = (src == 0 and dt == 0)
                        sp_ = (src == 1 and dt == 2)
                        wts = [qins[ic]]
                        if st_ and prev_evac is not None:
                            wts.append(prev_evac)
                        last = emit("tensor",
                                    lambda t, oc=oc, ic=ic, src=src, dt=dt, go=go, gw=gw, bk=bk, st_=st_, sp_=sp_:
                                    t.matmul(bank(bk, 0, 128, 0, gw), qc1_t(dt, oc, src),
                                             qin[ic][:, go + dt:go + dt + gw], start=st_, stop=sp_),
                                    waits=wts, inc=(cP if sp_ else None))
                prev_evac = emit("vector", lambda v, oc=oc, go=go, gw=gw, bk=bk: v.tensor_scalar(
                    q1o[oc][:, go:go + gw], bank(bk, 0, 128, 0, gw), qb_col(oc), None, ALU.add),
                    waits=[last], inc=cV)
        if sp == 0:
            for oc in range(8):
                emit("vector", lambda v, oc=oc: v.tensor_scalar(
                    q1o[oc][:, 0:2], q1o[oc][:, 0:2], ioddc, None, ALU.mult), inc=cV)
        q1_done = at(cV)
        prev_evac2 = None
        for oc in range(8):
            tl = [(dt, ic) for ic in range(8) for dt in range(3)]
            for idx, (dt, ic) in enumerate(tl):
                emit("gpsimd", lambda g, idx=idx, dt=dt, ic=ic, oc=oc: g.dma_start(
                    out=q2ring[:, idx * 128:(idx + 1) * 128], in_=qc2L[dt, oc, ic]),
                    waits=[prev_evac2] if idx == 0 else [], inc=dQR)
            lsn_all = at(dQR)
            last = None
            for idx, (dt, ic) in enumerate(tl):
                st_ = (idx == 0)
                wts = [lsn_all, q1_done]
                if st_ and prev_evac2 is not None:
                    wts.append(prev_evac2)
                last = emit("tensor", lambda t, idx=idx, dt=dt, ic=ic, st_=st_: t.matmul(
                    bank(3), q2ring[:, idx * 128:(idx + 1) * 128], q1o[ic][:, dt:dt + 512],
                    start=st_, stop=False),
                    waits=wts)
            last = emit("tensor", lambda t, oc=oc: t.matmul(bank(3), s5q_t(oc), km,
                                                            start=False, stop=True),
                        waits=[kme], inc=cP)
            prev_evac2 = emit("vector", lambda v, oc=oc: v.tensor_scalar(
                q2o[oc], bank(3), qb_col(8 + oc), None, ALU.add),
                waits=[last], inc=cV)
        q2_done = at(cV)
        osn = {}
        m_ = None
        for c in range(8):
            sq = emit("scalar", lambda s, c=c: s.activation(qsq[c % 2], q2o[c], AF.Square),
                      waits=[q2_done, osn.get(c - 2)], inc=cA)
            m_ = emit("tensor", lambda t, c=c: t.matmul(bank(5, 0, 16), oq_t(c), qsq[c % 2],
                                                        start=(c == 0), stop=(c == 7)),
                      waits=[sq], inc=cP)
            osn[c] = m_
        s1_ = emit("scalar", lambda s: s.activation(rnq, bank(5, 0, 16), AF.Sqrt), waits=[m_], inc=cA)
        emit("vector", lambda v: v.tensor_scalar_max(rnq, rnq, 1e-12), waits=[s1_])
        v3_ = emit("vector", lambda v: v.reciprocal(rnq, rnq), inc=cV)
        prev_tt = None
        for ch in range(8):
            bb = emit("tensor", lambda t, ch=ch: t.matmul(bank(6), bq_t(ch), rnq, start=True, stop=True),
                      waits=[v3_] + ([prev_tt] if prev_tt else []), inc=cP)
            prev_tt = emit("vector", lambda v, ch=ch: v.tensor_tensor(q2o[ch], q2o[ch], bank(6), ALU.mult),
                           waits=[bb], inc=cV)
        qh_done = at(cV)
        cksb = AR2[:, 9280 + 1024:9280 + 1536]
        sksb = AR2[:, 9280 + 1536:9280 + 2048]
        tmp1 = AR2[:, 9280 + 2048:9280 + 2560]
        tmp2 = AR2[:, 9280 + 2560:9280 + 3072]
        c1 = emit("gpsimd", lambda g, t0=t0: g.dma_start(out=cksb, in_=cosk[:, t0:t0 + TSP]),
                  inc=dcs[0])
        c2 = emit("gpsimd", lambda g, t0=t0: g.dma_start(out=sksb, in_=sink[:, t0:t0 + TSP]),
                  inc=dcs[1])
        prev_st = None
        for pc in range(4):
            e, o = q2o[pc], q2o[pc + 4]
            emit("vector", lambda v, e=e: v.tensor_tensor(tmp1, e, cksb, ALU.mult),
                 waits=[qh_done, c1, c2] + ([prev_st] if prev_st else []))
            emit("vector", lambda v, o=o: v.tensor_tensor(tmp2, o, sksb, ALU.mult))
            emit("vector", lambda v: v.tensor_tensor(qro[0], tmp1, tmp2, ALU.subtract))
            emit("vector", lambda v, e=e: v.tensor_tensor(tmp1, e, sksb, ALU.mult))
            emit("vector", lambda v, o=o: v.tensor_tensor(tmp2, o, cksb, ALU.mult))
            rp = emit("vector", lambda v: v.tensor_tensor(qro[1], tmp1, tmp2, ALU.add), inc=cV)
            emit("gpsimd", lambda g, pc=pc, t0=t0: g.dma_start(
                out=qfin[pc * 128:(pc + 1) * 128, t0:t0 + TSP], in_=qro[0]), waits=[rp], inc=cX)
            prev_st = emit("gpsimd", lambda g, pc=pc, t0=t0: g.dma_start(
                out=qfin[(4 + pc) * 128:(5 + pc) * 128, t0:t0 + TSP], in_=qro[1]),
                waits=[rp], inc=cX)
        barrier()

    def kst_b(p0, p1, kvh, e0, e1):
        return AR1[p0:p1, kvh * 1024:(kvh + 1) * 1024].bitcast(BF16)[:, e0:e1]

    for kvh in range(4):
        emit("gpsimd", lambda g, kvh=kvh: g.dma_start(
            out=kst_b(0, 32, kvh, 0, 2048), in_=kfin[32 * kvh:32 * kvh + 32, :]),
            inc=dKS)
        emit("gpsimd", lambda g, kvh=kvh: g.dma_start(
            out=kst_b(32, 64, kvh, 0, 2048),
            in_=kfin[128 + 32 * kvh:128 + 32 * kvh + 32, :]), inc=dKS)
    kst_ready = at(dKS)
    aout = AR2[0:64, 6208:6208 + 16 * 256].bitcast(BF16)
    for sp in range(OWN // TSP):
        t0 = sp * TSP
        qst = [AR1[:, 8192 + i * 256:8192 + (i + 1) * 256].bitcast(BF16) for i in range(2)]
        expb = [AR1[:, 9216 + i * 256:9216 + (i + 1) * 256].bitcast(BF16) for i in range(2)]
        maskb = [AR1[:, 10240 + i * 512:10240 + (i + 1) * 512] for i in range(2)]
        avs = AR1[0:64, 11264:11776]
        x1b = [AR1[:, 11776 + i * 512:11776 + (i + 1) * 512] for i in range(2)]
        x2b = [AR1[:, 12800 + i * 512:12800 + (i + 1) * 512] for i in range(2)]
        qfree = {}
        mask_done = {}
        exp_done = {}
        avlast = None
        zblast = None
        for h in range(NH):
            kv = h // 4
            gi0 = h * 16
            emit("gpsimd", lambda g, h=h, t0=t0: g.dma_start(
                out=qst[h % 2][0:32, :], in_=qfin[32 * h:32 * h + 32, t0:t0 + TSP]),
                waits=[kst_ready, qfree.get(h - 2)], inc=qs2[h % 2])
            emit("gpsimd", lambda g, h=h, t0=t0: g.dma_start(
                out=qst[h % 2][32:64, :], in_=qfin[512 + 32 * h:512 + 32 * h + 32, t0:t0 + TSP]),
                inc=qs2[h % 2])
            qrdy = at(qs2[h % 2])
            for ci in range(16):
                gi = gi0 + ci
                sb_ = ci % 2
                msn = emit("gpsimd", lambda g, ci=ci, sb_=sb_, t0=t0: g.dma_start(
                    out=maskb[sb_], in_=maskin[ci][:, t0:t0 + TSP]),
                    waits=[mask_done.get(gi - 2)], inc=mk2[sb_])
                sc = emit("tensor", lambda t, ci=ci, kv=kv, sb_=sb_, h=h: t.matmul(
                    bank(sb_), kst_b(0, 64, kv, ci * 128, (ci + 1) * 128),
                    qst[h % 2][0:64, :], start=True, stop=True, tile_position=(0, 0)),
                    waits=[qrdy, exp_done.get(gi - 2)], inc=cP)
                ex = emit("scalar", lambda s, sb_=sb_: s.activation(expb[sb_], bank(sb_), AF.Exp,
                                                                    bias=negM),
                          waits=[sc], inc=cA)
                exp_done[gi] = ex
                mm_ = emit("vector", lambda v, sb_=sb_: v.tensor_tensor(expb[sb_], expb[sb_],
                                                                        maskb[sb_], ALU.mult),
                           waits=[ex, msn], inc=cV)
                mask_done[gi] = mm_
                slot = ci * 4 + kv
                avw = [mm_]
                if ci == 0 and avlast is not None:
                    avw.append(avlast)
                av = emit("tensor", lambda t, slot=slot, sb_=sb_, ci=ci: t.matmul(
                    bank(3, 0, 65), vone_b(slot, 0, 65), expb[sb_],
                    start=(ci == 0), stop=(ci == 15)),
                    waits=avw, inc=(cP if ci == 15 else None))
                if ci == 15:
                    av_done = av
            qfree[h] = at(cP)
            zr = PRow[0:1, 3072:3584]
            rz = emit("vector", lambda v: v.reciprocal(zr, bank(3, 64, 65)), waits=[av_done], inc=cV)
            zb = emit("tensor", lambda t: t.matmul(bank(4, 0, 64), ones1r[0:1, 0:64], zr,
                                                   start=True, stop=True),
                      waits=[rz] + ([zblast] if zblast else []), inc=cP)
            ae = emit("scalar", lambda s: s.copy(avs, bank(3, 0, 64)), waits=[rz], inc=cA)
            avlast = ae
            zblast = emit("vector", lambda v, h=h: v.tensor_tensor(
                aout[:, h * 512:(h + 1) * 512], avs, bank(4, 0, 64), ALU.mult),
                waits=[zb, ae], inc=cV)
        aout_done = at(cV)
        wob = [AR3[:, i * 384:(i + 1) * 384].bitcast(BF16) for i in range(4)]
        wofree = {}
        bank_free2 = [None] * 6
        xfree, x2free = {}, {}
        for wave in range(6):
            ncb = 6 if wave < 5 else 2
            for h in range(NH):
                i = wave * 16 + h
                wsn = emit("sync", lambda s, wave=wave, h=h, sl=i % 4, ncb=ncb: s.dma_start(
                    out=wob[sl][0:64, 0:ncb * 128],
                    in_=woT_b[64 * h:64 * h + 64, wave * 768:wave * 768 + ncb * 128]),
                    waits=[aout_done, wofree.get(i - 4)], inc=fw4[i % 4])
                for nb in range(ncb):
                    wts = [wsn]
                    if h == 0 and bank_free2[nb] is not None:
                        wts.append(bank_free2[nb])
                    sn = emit("tensor", lambda t, h=h, nb=nb, sl=i % 4: t.matmul(
                        bank(nb), wob[sl][0:64, nb * 128:(nb + 1) * 128],
                        aout[:, h * 512:(h + 1) * 512],
                        start=(h == 0), stop=(h == NH - 1), tile_position=(0, 0)),
                        waits=wts, inc=(cP if (nb == ncb - 1 or h == NH - 1) else None))
                    if h == NH - 1:
                        c = wave * 6 + nb
                        xs = emit("gpsimd", lambda g, c=c, t0=t0: g.dma_start(
                            out=x1b[c % 2], in_=x1[c * 128:(c + 1) * 128, t0:t0 + TSP]),
                            waits=[xfree.get(c - 2)], inc=resX[c % 2])
                        vv = emit("vector", lambda v, c=c, nb=nb: v.scalar_tensor_tensor(
                            x2b[c % 2], bank(nb), 1.0, x1b[c % 2], ALU.mult, ALU.add),
                            waits=[sn, xs, x2free.get(c - 2)], inc=cV)
                        bank_free2[nb] = vv
                        xfree[c] = vv
                        x2free[c] = emit("gpsimd", lambda g, c=c, t0=t0: g.dma_start(
                            out=x2[c * 128:(c + 1) * 128, t0:t0 + TSP], in_=x2b[c % 2]),
                            waits=[vv], inc=st2[c % 2])
                wofree[i] = at(cP)
        barrier()

    for sp in range(OWN // TSP):
        t0 = sp * TSP
        ffn_span(lambda c, t0=t0: x2[c * 128:(c + 1) * 128, t0:t0 + TSP],
                 lambda c, t0=t0: y_out[c * 128:(c + 1) * 128, t0:t0 + TSP])
        barrier()

    emit("gpsimd", None, waits=[at(c) for c in DCHS])

    with nc.Block() as block:
        for ename in ENGINES:
            def make(ename):
                def run(eng):
                    for fn, waits, incspec in P.ops[ename]:
                        for sem, val in waits:
                            eng.wait_ge(sem, val)
                        if fn is None:
                            continue
                        instr = fn(eng)
                        if incspec is not None:
                            instr.then_inc(*incspec)
                return run
            getattr(block, ename)(make(ename))
    est.close()
    return nc



def _deint_perm(nheads):
    n = nheads * HD
    perm = np.zeros(n, dtype=np.int64)
    half = n // 2
    for h in range(nheads):
        for i in range(32):
            perm[h * 32 + i] = h * HD + 2 * i
            perm[half + h * 32 + i] = h * HD + 2 * i + 1
    return perm


def prep_inputs(inputs):
    f = lambda k: np.asarray(inputs[k], dtype=np.float32)
    x = f("L0_attn_x")
    wqkv = f("L0_attn_w_qkv")
    wo = f("L0_attn_w_o")
    qc1w, qc1b = f("L0_attn_qc1_w"), f("L0_attn_qc1_b")
    qc2w, qc2b = f("L0_attn_qc2_w"), f("L0_attn_qc2_b")
    kc1w, kc1b = f("L0_attn_kc1_w"), f("L0_attn_kc1_b")
    kc2w, kc2b = f("L0_attn_kc2_w"), f("L0_attn_kc2_b")
    ktemp = f("L0_attn_key_temp")
    w1, w1s = f("L0_ffn_w1"), f("L0_ffn_w1_scale")
    w3, w3s = f("L0_ffn_w3"), f("L0_ffn_w3_scale")
    w2, w2s = f("L0_ffn_w2"), f("L0_ffn_w2_scale")

    qperm = _deint_perm(NH)
    kperm = _deint_perm(NKV)
    rows = np.concatenate([qperm, LAT + kperm, np.arange(LAT + KVD, QKV)])
    wqkvT = np.ascontiguousarray(wqkv[rows, :].T)
    woT = np.ascontiguousarray(wo.T)
    w1T = np.ascontiguousarray(w1.T)
    w3T = np.ascontiguousarray(w3.T)
    w2T = np.ascontiguousarray(w2.T)
    s1row = np.ascontiguousarray(np.repeat(w1s, 128, axis=0).T)
    s3row = np.ascontiguousarray(np.repeat(w3s, 128, axis=0).T)
    s2row = np.ascontiguousarray(np.repeat(w2s, 128, axis=0).T)

    def conv1_tiles(w, b, nheads, perm):
        C = nheads * HD
        nch = C // 128
        nch_half = (C // 2) // 128
        dense = np.zeros((C, C, 3), dtype=np.float32)
        for o in range(C):
            g = o // HD
            dense[o, g * HD:(g + 1) * HD, :] = w[o]
        dd = dense[perm][:, perm]
        tiles = np.zeros((3, nch, 2, 128, 128), dtype=np.float32)
        for dt in range(3):
            for oc in range(nch):
                for src in range(2):
                    ic = (oc % nch_half) + src * nch_half
                    tiles[dt, oc, src] = dd[oc * 128:(oc + 1) * 128,
                                            ic * 128:(ic + 1) * 128, dt].T
        return np.ascontiguousarray(tiles), np.ascontiguousarray(b[perm])

    def conv2_tiles(w, b, C, perm):
        nch = C // 128
        dd = w[perm][:, perm]
        tiles = np.zeros((3, nch, nch, 128, 128), dtype=np.float32)
        for dt in range(3):
            for oc in range(nch):
                for ic in range(nch):
                    tiles[dt, oc, ic] = dd[oc * 128:(oc + 1) * 128,
                                           ic * 128:(ic + 1) * 128, dt].T
        return np.ascontiguousarray(tiles), np.ascontiguousarray(b[perm])

    qc1L, qb1 = conv1_tiles(qc1w, qc1b, NH, qperm)
    qc2L, qb2 = conv2_tiles(qc2w, qc2b, LAT, qperm)
    kc1L, kb1 = conv1_tiles(kc1w, kc1b, NKV, kperm)
    kc2L, kb2 = conv2_tiles(kc2w, kc2b, KVD, kperm)
    qbias = np.zeros((128, 16), dtype=np.float32)
    for c in range(8):
        qbias[:, c] = qb1[c * 128:(c + 1) * 128]
        qbias[:, 8 + c] = qb2[c * 128:(c + 1) * 128]
    kbias = np.zeros((128, 4), dtype=np.float32)
    for c in range(2):
        kbias[:, c] = kb1[c * 128:(c + 1) * 128]
        kbias[:, 2 + c] = kb2[c * 128:(c + 1) * 128]

    Mq = np.zeros((8, 128, 64), dtype=np.float32)
    Mk = np.zeros((2, 128, 64), dtype=np.float32)
    S5q = np.zeros((8, 64, 128), dtype=np.float32)
    S5k = np.zeros((2, 64, 128), dtype=np.float32)
    Oq = np.zeros((8, 128, 16), dtype=np.float32)
    Ok = np.zeros((2, 128, 4), dtype=np.float32)
    Bq = np.zeros((8, 16, 128), dtype=np.float32)
    Bk = np.zeros((2, 4, 128), dtype=np.float32)
    for c in range(8):
        for r in range(128):
            i = r % 32
            d0 = 0 if c < 4 else 32
            Mq[c, r, d0 + i] = 1.0 / NH
            S5q[c, d0 + i, r] = 0.5
            h = (c % 4) * 4 + r // 32
            Oq[c, r, h] = 1.0
            Bq[c, h, r] = 1.0
    for c in range(2):
        for r in range(128):
            i = r % 32
            d0 = 0 if c < 1 else 32
            Mk[c, r, d0 + i] = 1.0 / NKV
            S5k[c, d0 + i, r] = 0.5
            h = r // 32
            Ok[c, r, h] = 1.0
            Bk[c, h, r] = 1.0

    ident = np.eye(128, dtype=np.float32)
    shared = dict(wqkvT=wqkvT, woT=woT, w1T=w1T, w3T=w3T, w2T=w2T,
                  s1row=s1row, s3row=s3row, s2row=s2row,
                  qc1L=qc1L, qc2L=qc2L, kc1L=kc1L, kc2L=kc2L,
                  qbias=qbias, kbias=kbias, Mq=Mq, Mk=Mk, S5q=S5q, S5k=S5k,
                  Oq=Oq, Ok=Ok, Bq=Bq, Bk=Bk, ident=ident,
                  ktemp=np.broadcast_to(ktemp.reshape(1, 1), (128, 1)).copy())

    inv = 1.0 / (10000.0 ** (np.arange(0, HD, 2, dtype=np.float32) / HD))
    ang = np.arange(S, dtype=np.float32)[:, None] * inv[None, :]
    cosg = np.cos(ang).T.astype(np.float32)
    sing = np.sin(ang).T.astype(np.float32)

    in_maps = []
    for core in range(8):
        b, h = core // 2, core % 2
        off = h * OWN
        order = np.concatenate([np.arange(off, off + OWN), np.arange(0, off),
                                np.arange(off + OWN, S)])
        gt = order.astype(np.float32)
        m = dict(shared)
        m["x_fm"] = np.ascontiguousarray(x[b].T[:, order])
        m["cosk"] = np.ascontiguousarray(np.tile(cosg[:, order], (4, 1)))
        m["sink"] = np.ascontiguousarray(np.tile(sing[:, order], (4, 1)))
        maskin = np.zeros((16, 128, OWN), dtype=np.float32)
        for ci in range(16):
            maskin[ci] = (gt[ci * 128:(ci + 1) * 128][:, None] <= gt[None, :OWN]).astype(np.float32)
        m["maskin"] = maskin
        m["iodd"] = np.full((128, 1), float(h == 1), dtype=np.float32)
        m["ieven"] = np.full((128, 1), float(h == 0), dtype=np.float32)
        in_maps.append(m)
    return in_maps


def kernel(**inputs):
    in_maps = prep_inputs(inputs)
    nc = build_program()
    res = run_bass_kernel_spmd(nc, in_maps, core_ids=list(range(8)))
    out = np.zeros((4, S, D), dtype=np.float32)
    for core in range(8):
        b, h = core // 2, core % 2
        y = res.results[core]["y"]
        out[b, h * OWN:(h + 1) * OWN, :] = np.asarray(y).T
    return out
